# version 12
# baseline (speedup 1.0000x reference)
import contextlib
import numpy as np
import concourse.bass as bass
import concourse.mybir as mybir
from concourse.bass_utils import run_bass_kernel_spmd

F32 = mybir.dt.float32
BF16 = mybir.dt.bfloat16
AF = mybir.ActivationFunctionType
ALU = mybir.AluOpType
AX = mybir.AxisListType

S, D, H, HD, E, FP = 4096, 1024, 16, 64, 16, 256
NT = S // 128
NQ = S // 512
NCH = D // 128
NST = S // 512
EPS = 1e-5

ENGINES = ("sync", "gpsimd", "tensor", "vector", "scalar")


class Prog:
    def __init__(self, sems):
        self.q = {e: [] for e in ENGINES}
        self.sems = sems
        self.cnt = {n: 0 for n in sems}
        self.lastwait = {e: {n: 0 for n in sems} for e in ENGINES}

    def inc(self, eng, sem, fn, by=1):
        self.cnt[sem] += by
        val = self.cnt[sem]
        s = self.sems[sem]
        self.q[eng].append(lambda EE, fn=fn, s=s, by=by: fn(EE).then_inc(s, by))
        return (sem, val)

    def emit(self, eng, fn):
        self.q[eng].append(fn)

    def tok(self, sem):
        return (sem, self.cnt[sem])

    def wait(self, eng, tok):
        if tok is None:
            return
        sem, val = tok
        if self.lastwait[eng][sem] >= val:
            return
        self.lastwait[eng][sem] = val
        s = self.sems[sem]
        self.q[eng].append(lambda EE, s=s, v=val: EE.wait_ge(s, v))

    def barrier(self):
        for d in self.dma_sems:
            self.wait("sync", self.tok(d))
            self.wait("gpsimd", self.tok(d))
        toks = [self.inc(e, "bar", lambda EE: EE.nop()) for e in ENGINES]
        for e in ENGINES:
            for t in toks:
                self.wait(e, t)


def build_nc():
    nc = bass.Bass("TRN2", target_bir_lowering=False)
    dtn = nc.dram_tensor
    x0 = dtn("x0", [S, D], F32, kind="ExternalInput").ap()
    wqkvT = dtn("wqkvT", [24, 8, 128, 128], F32, kind="ExternalInput").ap()
    wbT = dtn("wbT", [128, 24], F32, kind="ExternalInput").ap()
    projT = dtn("projT", [8, 128, D], F32, kind="ExternalInput").ap()
    projb = dtn("projb", [1, D], F32, kind="ExternalInput").ap()
    lng = dtn("lng", [1, D], F32, kind="ExternalInput").ap()
    lnb = dtn("lnb", [1, D], F32, kind="ExternalInput").ap()
    cbias = dtn("cbias", [H, 128, 4480], F32, kind="ExternalInput").ap()
    wg_in = dtn("wg", [8, 128, E], F32, kind="ExternalInput").ap()
    bg_in = dtn("bg", [1, E], F32, kind="ExternalInput").ap()
    w1_in = dtn("w1", [E, 8, 128, FP], F32, kind="ExternalInput").ap()
    b1T_in = dtn("b1T", [128, 32], F32, kind="ExternalInput").ap()
    w2_in = dtn("w2", [E, 2, 128, D], F32, kind="ExternalInput").ap()
    b2_in = dtn("b2", [E, D], F32, kind="ExternalInput").ap()
    gam = dtn("gam", [1, D], F32, kind="ExternalInput").ap()
    bet = dtn("bet", [1, D], F32, kind="ExternalInput").ap()
    c_idf = dtn("c_idf", [128, 128], F32, kind="ExternalInput").ap()
    c_tri = dtn("c_tri", [5, 128, 512], F32, kind="ExternalInput").ap()
    c_e16 = dtn("c_e16", [16, 2048], F32, kind="ExternalInput").ap()
    out = dtn("out", [S, D], F32, kind="ExternalOutput").ap()
    x1sp = dtn("x1sp", [S, D], F32, kind="Internal").ap()
    x2sp = dtn("x2sp", [S, D], F32, kind="Internal").ap()
    utsp = dtn("utsp", [S, D], BF16, kind="Internal").ap()
    w1bf = dtn("w1bf", [E, 8, 128, FP], BF16, kind="Internal").ap()
    w2bf = dtn("w2bf", [E, 2, 128, D], BF16, kind="Internal").ap()
    osp = dtn("osp", [NCH, NT, 128, 128], BF16, kind="Internal").ap()

    es = contextlib.ExitStack()
    with es:
        def sbIn(stack, nm, shp, dtp=F32):
            return stack.enter_context(nc.sbuf_tensor(nm, shp, dtp))

        sb = lambda nm, shp, dtp=F32: sbIn(es, nm, shp, dtp)
        sems = {}
        DMA_SEMS = (["dPreS", "dPreG", "dos", "dpj"]
                    + ["dWp%d" % i for i in range(4)]
                    + ["dxa%d" % i for i in range(4)]
                    + ["dup%d" % i for i in range(2)]
                    + ["dw1%d" % i for i in range(2)]
                    + ["dw2%d" % i for i in range(2)]
                    + ["dut%d" % i for i in range(4)]
                    + ["dxp%d" % i for i in range(4)]
                    + ["dxo%d" % i for i in range(2)]
                    + ["dwsl%d" % i for i in range(3)]
                    + ["draw%d" % i for i in range(2)]
                    + ["dot%d" % i for i in range(2)])
        for n in ["sP", "sV", "sA", "sG", "bar"] + DMA_SEMS:
            sems[n] = es.enter_context(nc.semaphore(n))
        p = Prog(sems)
        p.dma_sems = DMA_SEMS

        u_sb = sb("u_sb", [128, NCH * S], BF16)
        u_ch = u_sb[:].rearrange("p (c s) -> p c s", c=NCH)
        idf = sb("idf", [128, 128], F32)
        idb = sb("idb", [128, 128], BF16)
        ones_bf = sb("ones_bf", [128, 128], BF16)
        ones1f = sb("ones1f", [1, 128], F32)
        epst = sb("epst", [128, 1], F32)
        mhalf = sb("mhalf", [128, 1], F32)
        mh_nt = sb("mh_nt", [128, NT], F32)
        ph16 = sb("ph16", [128, 16], F32)
        b1Ts = sb("b1Ts", [128, 32], F32)
        wbTs = sb("wbTs", [128, 24], F32)
        wbTq = sb("wbTq", [128, 8], F32)
        b2sb = sb("b2sb", [16, D], BF16)
        wgsb = sb("wgsb", [128, 8 * E], F32)
        wg3 = wgsb[:].rearrange("p (c e) -> p c e", c=8)
        bgsb = sb("bgsb", [1, E], F32)
        gbc = sb("gbc", [128, D], BF16)
        bbc = sb("bbc", [128, D], BF16)
        lgbc = sb("lgbc", [128, D], BF16)
        lbbc = sb("lbbc", [128, D], BF16)
        pjbc = sb("pjbc", [128, D], BF16)
        selmT = sb("selmT", [16, S], BF16)
        R_all = sb("R_all", [128, NT * E], BF16)
        Z_all = sb("Z_all", [128, NT * E], BF16)
        mv_all = sb("mv_all", [128, 2 * NT], F32)
        mv3 = mv_all[:].rearrange("p (t two) -> p t two", two=2)
        rstd_a = sb("rstd_a", [128, NT], F32)
        nmur_a = sb("nmur_a", [128, NT], F32)
        ve_a = sb("ve_a", [128, NT], F32)
        xp = [sb("xp%d" % i, [128, D], F32) for i in range(4)]
        xo = [sb("xo0", [128, D], F32), sb("xo1", [128, D], F32)]
        utk = [sb("utk%d" % i, [128, D], BF16) for i in range(4)]
        xhat = [sb("xhat0", [128, D], BF16), sb("xhat1", [128, D], BF16)]
        st12 = sb("st12", [128, 12], F32)
        mv2 = sb("mv2", [128, 2], F32)
        ve2 = sb("ve2", [128, 1], F32)
        rs2 = sb("rs2", [128, 1], F32)
        nm2 = sb("nm2", [128, 1], F32)
        sqv = sb("sqv", [128, 1], F32)
        lg2 = sb("lg2", [128, E], F32)
        lgt = sb("lgt", [128, E], F32)
        m1t = sb("m1t", [128, 1], F32)
        m2t = sb("m2t", [128, 1], F32)
        pen = sb("pen", [128, E], F32)
        ne_sb = sb("ne_sb", [1, E], F32)
        fill = sb("fill", [E, 1], F32)
        cmp_t = sb("cmp_t", [E, 512], F32)

        pre = []

        def preS(fn):
            p.wait("sync", p.tok("dPreS"))
            pre.append(p.inc("sync", "dPreS", fn, 16))

        def preG(fn):
            p.wait("gpsimd", p.tok("dPreG"))
            pre.append(p.inc("gpsimd", "dPreG", fn, 16))

        preS(lambda EE: EE.dma_start(out=idf[:], in_=c_idf))
        preG(lambda EE: EE.dma_start(out=idb[:], in_=c_idf))
        preS(lambda EE: EE.dma_start(out=b1Ts[:], in_=b1T_in))
        preS(lambda EE: EE.dma_start(out=wbTs[:], in_=wbT))
        preG(lambda EE: EE.dma_start(out=b2sb[:], in_=b2_in))
        preS(lambda EE: EE.dma_start(out=wg3, in_=wg_in.rearrange("c p e -> p c e")))
        preS(lambda EE: EE.dma_start(out=bgsb[:], in_=bg_in))
        for tk in pre:
            for e_ in ENGINES:
                p.wait(e_, tk)
        p.emit("gpsimd", lambda EE: EE.memset(ones_bf[:], 1.0))
        p.emit("gpsimd", lambda EE: EE.memset(ones1f[:], 1.0))
        p.emit("gpsimd", lambda EE: EE.memset(epst[:], EPS))
        p.emit("gpsimd", lambda EE: EE.memset(mhalf[:], -0.5))
        p.emit("gpsimd", lambda EE: EE.memset(mh_nt[:], -0.5))
        tG0 = p.inc("gpsimd", "sG", lambda EE: EE.memset(ph16[:], 0.5))
        for e_ in ENGINES:
            p.wait(e_, tG0)
        tV0 = p.inc("vector", "sV", lambda EE: EE.tensor_scalar(
            wbTq[:], wbTs[:, 0:8], 1.0 / 32.0, None, ALU.mult, ALU.bypass))
        p.wait("scalar", tV0)

        with contextlib.ExitStack() as ps0:
            pbc0 = ps0.enter_context(nc.psum_tensor("pbc0", [128, 512], F32))
            rowbuf = sbIn(ps0, "rowbuf", [1, 5 * D], F32)
            rbt = []
            for i, src in enumerate([gam, bet, lng, lnb, projb]):
                p.wait("sync", p.tok("dPreS"))
                rbt.append(p.inc("sync", "dPreS", lambda EE, i=i, src=src: EE.dma_start(
                    out=rowbuf[:, i * D:(i + 1) * D], in_=src), 16))
            for i, (dst, sc) in enumerate([(gbc, 0.5), (bbc, 0.5), (lgbc, 1.0),
                                           (lbbc, 1.0), (pjbc, 1.0)]):
                p.wait("tensor", rbt[i])
                for h2 in range(2):
                    tmm = p.inc("tensor", "sP", lambda EE, i=i, h2=h2: EE.matmul(
                        pbc0[:], ones1f[:],
                        rowbuf[0:1, i * D + h2 * 512:i * D + (h2 + 1) * 512],
                        start=True, stop=True))
                    p.wait("scalar", tmm)
                    tev = p.inc("scalar", "sA", lambda EE, dst=dst, sc=sc, h2=h2: EE.activation(
                        dst[:, h2 * 512:(h2 + 1) * 512], pbc0[:], AF.Copy, scale=sc))
                    p.wait("tensor", tev)
        p.barrier()
        wst4 = [sb("wst%d" % i, [128, 2048], BF16) for i in range(2)]
        wtoks = {}

        def emit_prepass():
            stok = [None] * 4
            for e in range(E):
                for w_i, (src, dst3) in enumerate([
                        (w1_in[e].rearrange("c p f -> p c f"),
                         w1bf[e].rearrange("c p f -> p c f")),
                        (w2_in[e].rearrange("c p d_ -> p c d_"),
                         w2bf[e].rearrange("c p d_ -> p c d_"))]):
                    sl = (2 * e + w_i) % 2
                    nchp = 8 if w_i == 0 else 2
                    p.wait("gpsimd", stok[sl])
                    tl_ = p.inc("gpsimd", "dWp%d" % sl, lambda EE, src=src, sl=sl, nchp=nchp: EE.dma_start(
                        out=wst4[sl][:].rearrange("p (c f) -> p c f", c=nchp), in_=src), 16)
                    p.wait("gpsimd", tl_)
                    stok[sl] = p.inc("gpsimd", "dWp%d" % sl, lambda EE, dst3=dst3, sl=sl, nchp=nchp: EE.dma_start(
                        out=dst3, in_=wst4[sl][:].rearrange("p (c f) -> p c f", c=nchp)), 16)
                    wtoks[(e, w_i)] = stok[sl]

        def phase1(xin, gate, sfx, xa, ub, uT32, upk):
            with contextlib.ExitStack() as ps:
                ptr = ps.enter_context(nc.psum_tensor("ptr" + sfx, [128, D], F32))
                pgate = (ps.enter_context(nc.psum_tensor("pg" + sfx, [128, E], F32))
                         if gate else None)
                xfree = [None] * 4
                ubfree = [None, None]
                uTfree = [None, None]
                ptrfree = None
                upk_st = [None, None]
                gate_tok = None
                lgt_tok = None
                stat_tok = {}
                for it in range(NT + 2):
                    if it < NT:
                        t = it
                        b = t % 4
                        p.wait("sync", xfree[b])
                        tl = p.inc("sync", "dxa%d" % b, lambda EE, t=t, b=b: EE.dma_start(
                            out=xa[b][:], in_=xin[t * 128:(t + 1) * 128, :]), 16)
                        p.wait("vector", tl)
                        t1 = p.inc("vector", "sV", lambda EE, b=b: EE.bn_stats(
                            st12[:, 0:6], xa[b][:, 0:512]))
                        p.wait("vector", t1)
                        t2 = p.inc("vector", "sV", lambda EE, b=b: EE.bn_stats(
                            st12[:, 6:12], xa[b][:, 512:1024]))
                        p.wait("vector", t2)
                        t3 = p.inc("vector", "sV", lambda EE: EE.bn_aggr(mv2[:], st12[:]))
                        p.wait("vector", t3)
                        t4 = p.inc("vector", "sV", lambda EE: EE.tensor_scalar(
                            ve2[:], mv2[:, 1:2], epst[:], None, ALU.add, ALU.bypass))
                        p.wait("scalar", t4)
                        t5 = p.inc("scalar", "sA", lambda EE: EE.activation(
                            sqv[:], ve2[:], AF.Sqrt))
                        p.wait("vector", t5)
                        t6 = p.inc("vector", "sV", lambda EE, t=t: EE.reciprocal(
                            rstd_a[:, t:t + 1], sqv[:]))
                        p.wait("vector", t6)
                        t7 = p.inc("vector", "sV", lambda EE, t=t: EE.tensor_tensor(
                            nmur_a[:, t:t + 1], mv2[:, 0:1], rstd_a[:, t:t + 1], ALU.mult))
                        p.wait("vector", t7)
                        t8 = p.inc("vector", "sV", lambda EE, t=t: EE.tensor_scalar(
                            nmur_a[:, t:t + 1], nmur_a[:, t:t + 1], -1.0, None,
                            ALU.mult, ALU.bypass))
                        stat_tok[t] = t8
                    if it < 2:
                        continue
                    t = it - 2
                    b = t % 4
                    b2_ = t % 2
                    p.wait("vector", stat_tok[t])
                    p.wait("vector", ubfree[b2_])
                    tu = p.inc("vector", "sV", lambda EE, t=t, b=b, b2_=b2_: EE.tensor_scalar(
                        ub[b2_][:], xa[b][:], rstd_a[:, t:t + 1], nmur_a[:, t:t + 1],
                        ALU.mult, ALU.add))
                    xfree[b] = tu
                    p.wait("sync", upk_st[b2_])
                    p.wait("vector", tu)
                    tuk = p.inc("vector", "sV", lambda EE, b2_=b2_: EE.tensor_copy(
                        upk[b2_][:], ub[b2_][:]))
                    p.wait("sync", tuk)
                    tst_ = p.inc("sync", "dup%d" % b2_, lambda EE, t=t, b2_=b2_: EE.dma_start(
                        out=utsp[t * 128:(t + 1) * 128, :], in_=upk[b2_][:]), 16)
                    upk_st[b2_] = tst_
                    p.wait("tensor", tu)
                    p.wait("tensor", ptrfree)
                    for c in range(NCH):
                        p.inc("tensor", "sP", lambda EE, b2_=b2_, c=c: EE.transpose(
                            ptr[:, c * 128:(c + 1) * 128],
                            ub[b2_][:, c * 128:(c + 1) * 128], idf[:]))
                    ttr = p.tok("sP")
                    ubfree[b2_] = ttr
                    p.wait("scalar", ttr)
                    p.wait("scalar", uTfree[b2_])
                    tev = p.inc("scalar", "sA", lambda EE, b2_=b2_: EE.activation(
                        uT32[b2_][:], ptr[:], AF.Copy))
                    ptrfree = tev
                    p.wait("vector", tev)
                    tub = p.inc("vector", "sV", lambda EE, t=t, b2_=b2_: EE.tensor_copy(
                        u_ch[:, :, t * 128:(t + 1) * 128],
                        uT32[b2_][:].rearrange("p (c f) -> p c f", c=NCH)))
                    uTfree[b2_] = tub
                    if gate:
                        p.wait("tensor", tev)
                        p.wait("tensor", gate_tok)
                        for c in range(NCH):
                            p.inc("tensor", "sP", lambda EE, b2_=b2_, c=c: EE.matmul(
                                pgate[:], uT32[b2_][:, c * 128:(c + 1) * 128],
                                wg3[:, c, :], start=(c == 0), stop=False))
                        tgm = p.inc("tensor", "sP", lambda EE: EE.matmul(
                            pgate[:], ones1f[:], bgsb[:], start=False, stop=True))
                        uTfree[b2_] = tgm
                        p.wait("vector", tgm)
                        p.wait("vector", gate_tok)
                        p.wait("vector", lgt_tok)
                        tcp = p.inc("vector", "sV", lambda EE: EE.tensor_copy(
                            lgt[:], pgate[:]))
                        gate_tok = tcp
                        p.wait("vector", tcp)
                        g1 = p.inc("vector", "sV", lambda EE: EE.reduce_max(
                            m1t[:], lgt[:], axis=AX.X))
                        p.wait("vector", g1)
                        g2 = p.inc("vector", "sV", lambda EE: EE.tensor_scalar(
                            pen[:], lgt[:], m1t[:], -1e30, ALU.is_ge, ALU.mult))
                        p.wait("vector", g2)
                        g3 = p.inc("vector", "sV", lambda EE: EE.tensor_tensor(
                            lg2[:], lgt[:], pen[:], ALU.add))
                        p.wait("vector", g3)
                        g4 = p.inc("vector", "sV", lambda EE: EE.reduce_max(
                            m2t[:], lg2[:], axis=AX.X))
                        p.wait("vector", g4)
                        g5 = p.inc("vector", "sV", lambda EE, t=t: EE.tensor_scalar(
                            R_all[:, t * E:(t + 1) * E], lgt[:], m2t[:], None,
                            ALU.is_ge, ALU.bypass))
                        p.wait("vector", g5)
                        p.inc("vector", "sV", lambda EE, t=t: EE.tensor_scalar(
                            Z_all[:, t * E:(t + 1) * E], R_all[:, t * E:(t + 1) * E],
                            -1.0, 1.0, ALU.mult, ALU.add))
                        lgt_tok = g5

        def routing_finalize(sfx):
            p.wait("tensor", p.tok("sV"))
            p.wait("tensor", p.tok("sG"))
            with contextlib.ExitStack() as ps:
                pne = ps.enter_context(nc.psum_tensor("pne" + sfx, [1, E], F32))
                pne2 = ps.enter_context(nc.psum_tensor("pne2" + sfx, [E, 1], F32))
                pzc = ps.enter_context(nc.psum_tensor("pzc" + sfx, [E, 512], F32))
                pRT = ps.enter_context(nc.psum_tensor("pRT" + sfx, [E, 512], BF16))
                for t in range(NT):
                    p.inc("tensor", "sP", lambda EE, t=t: EE.matmul(
                        pne[:], ones_bf[:, 0:1], R_all[:, t * E:(t + 1) * E],
                        start=(t == 0), stop=(t == NT - 1)))
                tne = p.tok("sP")
                p.wait("vector", tne)
                tnesb = p.inc("vector", "sV", lambda EE: EE.tensor_copy(ne_sb[:], pne[:]))
                p.wait("tensor", tnesb)
                tne2 = p.inc("tensor", "sP", lambda EE: EE.matmul(
                    pne2[:], ne_sb[:], ones1f[0:1, 0:1], start=True, stop=True))
                p.wait("vector", tne2)
                tfill = p.inc("vector", "sV", lambda EE: EE.tensor_scalar(
                    fill[:], pne2[:], -1.0, float(S // 2), ALU.mult, ALU.add))
                prevsel = None
                for tq in range(NQ):
                    nch_ = 4 * tq + 4
                    p.wait("tensor", prevsel)
                    for c in range(nch_):
                        pat = 4 if c < 4 * tq else (c - 4 * tq)
                        p.inc("tensor", "sP", lambda EE, c=c, pat=pat, nch_=nch_: EE.matmul(
                            pzc[:], Z_all[:, c * E:(c + 1) * E],
                            fb["tri5"][:, pat * 512:(pat + 1) * 512],
                            start=(c == 0), stop=(c == nch_ - 1)))
                    tzc = p.tok("sP")
                    for j in range(4):
                        c = 4 * tq + j
                        p.inc("tensor", "sP", lambda EE, c=c, j=j: EE.transpose(
                            pRT[:, j * 128:(j + 1) * 128], R_all[:, c * E:(c + 1) * E],
                            idb[:]))
                    tRT = p.tok("sP")
                    p.wait("vector", tzc)
                    p.wait("vector", tRT)
                    p.wait("vector", tfill)
                    tc = p.inc("vector", "sV", lambda EE: EE.tensor_scalar(
                        cmp_t[:], pzc[:], fill[:], None, ALU.is_le, ALU.bypass))
                    p.wait("vector", tc)
                    tsel = p.inc("vector", "sV", lambda EE, tq=tq: EE.tensor_tensor(
                        selmT[:, tq * 512:(tq + 1) * 512], cmp_t[:], pRT[:], ALU.max))
                    prevsel = tsel

        fb = {}
        ep_state = {"xp_free": [None] * 4, "xo_free": [None, None],
                    "xh_free": [None, None], "utk_free": [None] * 4}

        def emit_epilogue(pzt, tz, xin, xout, tt, g_bc, b_bc, final):
            b = tt % 2
            p.wait("vector", tz)
            t1 = p.inc("vector", "sV", lambda EE, pzt=pzt: EE.bn_stats(
                st12[:, 0:6], pzt[:, 0:512]))
            p.wait("vector", t1)
            t2 = p.inc("vector", "sV", lambda EE, pzt=pzt: EE.bn_stats(
                st12[:, 6:12], pzt[:, 512:1024]))
            p.wait("vector", t2)
            t3 = p.inc("vector", "sV", lambda EE: EE.bn_aggr(mv2[:], st12[:]))
            p.wait("vector", t3)
            t4 = p.inc("vector", "sV", lambda EE: EE.tensor_scalar(
                ve2[:], mv2[:, 1:2], epst[:], None, ALU.add, ALU.bypass))
            p.wait("gpsimd", t4)
            t5 = p.inc("gpsimd", "sG", lambda EE: EE.tensor_tensor(
                rs2[:], ve2[:], mhalf[:], ALU.pow))
            p.wait("vector", t5)
            t6 = p.inc("vector", "sV", lambda EE: EE.tensor_tensor(
                nm2[:], mv2[:, 0:1], rs2[:], ALU.mult))
            p.wait("vector", t6)
            t7 = p.inc("vector", "sV", lambda EE: EE.tensor_scalar(
                nm2[:], nm2[:], -1.0, None, ALU.mult, ALU.bypass))
            p.wait("vector", t7)
            p.wait("vector", ep_state["xh_free"][b])
            t8 = p.inc("vector", "sV", lambda EE, pzt=pzt, b=b: EE.tensor_scalar(
                xhat[b][:], pzt[:], rs2[:], nm2[:], ALU.mult, ALU.add))
            b4 = tt % 4
            p.wait("sync", ep_state["xp_free"][b4])
            tl = p.inc("sync", "dxp%d" % b4, lambda EE, tt=tt, b4=b4, xin=xin: EE.dma_start(
                out=xp[b4][:], in_=xin[tt * 128:(tt + 1) * 128, :]), 16)
            p.wait("vector", t8)
            t9 = p.inc("vector", "sV", lambda EE, b=b, g_bc=g_bc: EE.tensor_tensor(
                xhat[b][:], xhat[b][:], g_bc[:], ALU.mult))
            p.wait("vector", t9)
            t10 = p.inc("vector", "sV", lambda EE, b=b, b_bc=b_bc: EE.tensor_tensor(
                xhat[b][:], xhat[b][:], b_bc[:], ALU.add))
            p.wait("vector", tl)
            p.wait("vector", t10)
            p.wait("vector", ep_state["xo_free"][b])
            t11 = p.inc("vector", "sV", lambda EE, b=b, b4=b4: EE.tensor_tensor(
                xo[b][:], xp[b4][:], xhat[b][:], ALU.add))
            ep_state["xp_free"][b4] = t11
            ep_state["xh_free"][b] = t11
            p.wait("gpsimd", t11)
            tst = p.inc("gpsimd", "dxo%d" % b, lambda EE, tt=tt, b=b, xout=xout: EE.dma_start(
                out=xout[tt * 128:(tt + 1) * 128, :], in_=xo[b][:]), 16)
            ep_state["xo_free"][b] = tst
            return t8

        def ffn_phase2(xin, xout, final, sfx):
            with contextlib.ExitStack() as ps:
                w1s = [sbIn(ps, "w1s0" + sfx, [128, 8 * FP], BF16),
                       sbIn(ps, "w1s1" + sfx, [128, 8 * FP], BF16)]
                w2s = [sbIn(ps, "w2s0" + sfx, [128, 2 * D], BF16),
                       sbIn(ps, "w2s1" + sfx, [128, 2 * D], BF16)]
                Hm = sbIn(ps, "Hm" + sfx, [128, E * 512], BF16)
                Hm2 = sbIn(ps, "Hm2" + sfx, [128, E * 512], BF16)
                mkbc = sbIn(ps, "mkbc" + sfx, [128, 512], BF16)
                zsb = [sbIn(ps, "zsb0" + sfx, [128, D], F32),
                       sbIn(ps, "zsb1" + sfx, [128, D], F32)]
                zsb_free = [None, None]
                pz = [ps.enter_context(nc.psum_tensor("pz%d" % i + sfx, [128, 1024], F32))
                      for i in range(4)]
                w1ld = [None, None]
                w2ld = [None, None]
                ep_free = [None, None, None, None]
                gel_last = None
                mk_last = None
                w2_last = None
                for st in range(NST):
                    s0 = st * 512
                    gel_e = {}
                    for e in range(E):
                        wb_ = e % 2
                        if e == 0:
                            p.wait("tensor", ep_free[0])
                            p.wait("tensor", ep_free[1])
                        if e == 1:
                            p.wait("tensor", ep_free[2])
                            p.wait("tensor", ep_free[3])
                        p.wait("sync", wtoks[(e, 0)])
                        p.wait("sync", w1ld[wb_])
                        tw1 = p.inc("sync", "dw1%d" % wb_, lambda EE, e=e, wb_=wb_: EE.dma_start(
                            out=w1s[wb_][:].rearrange("p (c f) -> p c f", c=8),
                            in_=w1bf[e].rearrange("c p f -> p c f")), 16)
                        p.wait("tensor", tw1)
                        p.wait("tensor", gel_e.get(e - 2))
                        pza, pzb = (pz[0], pz[1]) if e % 2 == 0 else (pz[2], pz[3])
                        for f in range(2):
                            ptgt = pza if f == 0 else pzb
                            for c in range(NCH):
                                p.inc("tensor", "sP", lambda EE, wb_=wb_, f=f, c=c, s0=s0,
                                      ptgt=ptgt: EE.matmul(
                                    ptgt[:, 0:512],
                                    w1s[wb_][:].rearrange("p (c g) -> p c g", c=8)[:, c, f * 128:(f + 1) * 128],
                                    u_ch[:, c, s0:s0 + 512],
                                    start=(c == 0), stop=(c == NCH - 1)))
                        tw1d = p.tok("sP")
                        w1ld[wb_] = tw1d
                        p.wait("tensor", mk_last)
                        tmk = p.inc("tensor", "sP", lambda EE, e=e, s0=s0: EE.matmul(
                            pz[1][:, 512:1024], fb["e16"][:, e * 128:(e + 1) * 128],
                            selmT[:, s0:s0 + 512], start=True, stop=True))
                        p.wait("scalar", tw1d)
                        p.wait("scalar", w2_last)
                        for f in range(2):
                            dst = Hm if f == 0 else Hm2
                            ptgt = pza if f == 0 else pzb
                            p.inc("scalar", "sA", lambda EE, e=e, f=f, dst=dst, ptgt=ptgt: EE.activation(
                                dst[:, e * 512:(e + 1) * 512], ptgt[:, 0:512], AF.Gelu,
                                bias=b1Ts[:, e * 2 + f:e * 2 + f + 1]))
                        gel_last = p.tok("sA")
                        gel_e[e] = gel_last
                        p.wait("vector", tmk)
                        p.wait("vector", mk_last)
                        tmc = p.inc("vector", "sV", lambda EE: EE.tensor_copy(
                            mkbc[:], pz[1][:, 512:1024]))
                        mk_last = tmc
                        p.wait("vector", gel_last)
                        for f in range(2):
                            dst = Hm if f == 0 else Hm2
                            p.inc("vector", "sV", lambda EE, e=e, dst=dst: EE.tensor_tensor(
                                dst[:, e * 512:(e + 1) * 512],
                                dst[:, e * 512:(e + 1) * 512], mkbc[:], ALU.mult))
                        hm_all = p.tok("sV")
                    for e in range(E):
                        wb_ = e % 2
                        p.wait("sync", wtoks[(e, 1)])
                        p.wait("sync", w2ld[wb_])
                        tw2 = p.inc("sync", "dw2%d" % wb_, lambda EE, e=e, wb_=wb_: EE.dma_start(
                            out=w2s[wb_][:].rearrange("p (c d_) -> p c d_", c=2),
                            in_=w2bf[e].rearrange("c p d_ -> p c d_")), 16)
                        p.wait("tensor", tw2)
                        p.wait("tensor", hm_all)
                        if e == 0:
                            p.wait("tensor", gel_last)
                            p.wait("tensor", mk_last)
                        for ti in range(4):
                            for f in range(2):
                                src = Hm if f == 0 else Hm2
                                for dh in range(2):
                                    p.inc("tensor", "sP", lambda EE, e=e, wb_=wb_, f=f,
                                          dh=dh, ti=ti, src=src: EE.matmul(
                                        pz[ti][:, dh * 512:(dh + 1) * 512],
                                        src[:, e * 512 + ti * 128:e * 512 + (ti + 1) * 128],
                                        w2s[wb_][:].rearrange("p (c d_) -> p c d_", c=2)[:, f, dh * 512:(dh + 1) * 512],
                                        start=(e == 0 and f == 0), stop=False))
                        w2ld[wb_] = p.tok("sP")
                    w2_last = p.tok("sP")
                    for ti in range(4):
                        tt = st * 4 + ti
                        p.wait("sync", ep_state["utk_free"][ti])
                        tul = p.inc("sync", "dut%d" % ti, lambda EE, tt=tt, ti=ti: EE.dma_start(
                            out=utk[ti][:], in_=utsp[tt * 128:(tt + 1) * 128, :]), 16)
                        p.wait("tensor", tul)
                        for dh in range(2):
                            p.inc("tensor", "sP", lambda EE, ti=ti, dh=dh: EE.matmul(
                                pz[ti][:, dh * 512:(dh + 1) * 512], idb[:],
                                utk[ti][:, dh * 512:(dh + 1) * 512],
                                start=False, stop=False))
                            p.inc("tensor", "sP", lambda EE, ti=ti, dh=dh, tt=tt: EE.matmul(
                                pz[ti][:, dh * 512:(dh + 1) * 512],
                                selmT[:, tt * 128:(tt + 1) * 128],
                                b2sb[:, dh * 512:(dh + 1) * 512],
                                start=False, stop=True))
                        tz = p.tok("sP")
                        ep_state["utk_free"][ti] = tz
                        zb = ti % 2
                        p.wait("vector", tz)
                        p.wait("vector", zsb_free[zb])
                        tevz = p.inc("vector", "sV", lambda EE, ti=ti, zb=zb: EE.tensor_copy(
                            zsb[zb][:], pz[ti][:]))
                        ep_free[ti] = tevz
                        zsb_free[zb] = emit_epilogue(zsb[zb], tevz, xin, xout, tt,
                                                     gbc, bbc, final)

        def phase1_bufs(ps, sfx):
            xa = [sbIn(ps, "xa%d" % i + sfx, [128, D], F32) for i in range(4)]
            ub = [sbIn(ps, "ub0" + sfx, [128, D], F32), sbIn(ps, "ub1" + sfx, [128, D], F32)]
            uT32 = [sbIn(ps, "uT0" + sfx, [128, D], F32), sbIn(ps, "uT1" + sfx, [128, D], F32)]
            upk = [sbIn(ps, "upk0" + sfx, [128, D], BF16), sbIn(ps, "upk1" + sfx, [128, D], BF16)]
            return xa, ub, uT32, upk

        def ffn_call(xin, xout, final, sfx, post_stats=None):
            with contextlib.ExitStack() as phw:
                fb["tri5"] = sbIn(phw, "tri5" + sfx, [128, 5 * 512], BF16)
                fb["e16"] = sbIn(phw, "e16" + sfx, [16, 2048], BF16)
                p.wait("gpsimd", p.tok("dPreG"))
                t_ = p.inc("gpsimd", "dPreG", lambda EE: EE.dma_start(
                    out=fb["tri5"][:].rearrange("p (a q) -> p a q", a=5),
                    in_=c_tri.rearrange("a p q -> p a q")), 16)
                p.wait("gpsimd", t_)
                t2_ = p.inc("gpsimd", "dPreG", lambda EE: EE.dma_start(
                    out=fb["e16"][:], in_=c_e16), 16)
                p.wait("tensor", t_)
                p.wait("tensor", t2_)
                with contextlib.ExitStack() as ph:
                    xa, ub, uT32, upk = phase1_bufs(ph, sfx)
                    if post_stats is not None:
                        post_stats()
                    phase1(xin, True, sfx, xa, ub, uT32, upk)
                    routing_finalize(sfx)
                    p.barrier()
                ffn_phase2(xin, xout, final, sfx)
                p.barrier()

        def attn_call():
            with contextlib.ExitStack() as ph:
                xa, ub, uT32, upk = phase1_bufs(ph, "A")
                phase1(x1sp, False, "A", xa, ub, uT32, upk)
                p.barrier()
            with contextlib.ExitStack() as ps:
                QT = sbIn(ps, "QT", [128, S], BF16)
                KT = sbIn(ps, "KT", [128, S], BF16)
                Vt = sbIn(ps, "Vt", [128, 32 * 130], BF16)
                Vt3 = Vt[:].rearrange("p (k c) -> p k c", c=130)
                wsl = [sbIn(ps, "wslq", [128, 8 * 128], BF16),
                       sbIn(ps, "wslk", [128, 8 * 128], BF16),
                       sbIn(ps, "wslv", [128, 8 * 128], BF16)]
                biasB = sbIn(ps, "biasB", [128, 2 * 4480], BF16)
                vts = sbIn(ps, "vts", [128, S], BF16)
                qmax = sbIn(ps, "qmax", [128, 16], F32)
                kmax = sbIn(ps, "kmax", [128, 2], F32)
                kmtmp = sbIn(ps, "kmtmp", [128, 1], F32)
                negM = sbIn(ps, "negM", [128, 16], F32)
                mprod = sbIn(ps, "mprod", [128, 16], F32)
                Pt = [[sbIn(ps, "Pa0", [128, 512], BF16), sbIn(ps, "Pa1", [128, 512], BF16)],
                      [sbIn(ps, "Pb0", [128, 512], BF16), sbIn(ps, "Pb1", [128, 512], BF16)]]
                rec_f = sbIn(ps, "rec_f", [1, 1024], F32)
                rec_b = sbIn(ps, "rec_b", [1, 1024], BF16)
                recbc = sbIn(ps, "recbc", [128, 512], BF16)
                Ostg = sbIn(ps, "Ostg", [128, 512], BF16)
                pS = [[ps.enter_context(nc.psum_tensor("pSa0", [128, 512], F32)),
                       ps.enter_context(nc.psum_tensor("pSa1", [128, 512], F32))],
                      [ps.enter_context(nc.psum_tensor("pSb0", [128, 512], F32)),
                       ps.enter_context(nc.psum_tensor("pSb1", [128, 512], F32))]]
                pO = [ps.enter_context(nc.psum_tensor("pOa", [128, 512], F32)),
                      ps.enter_context(nc.psum_tensor("pOb", [128, 512], F32))]
                pbc = ps.enter_context(nc.psum_tensor("pbcA", [128, 512], F32))
                pvt = ps.enter_context(nc.psum_tensor("pvtA", [128, 1024], BF16))
                tvo = p.inc("gpsimd", "sG", lambda EE: EE.memset(Vt3[:, :, 64:65], 1.0))
                tvo2 = p.inc("gpsimd", "sG", lambda EE: EE.memset(Vt3[:, :, 129:130], 1.0))
                p.wait("tensor", tvo2)
                wfree = None
                prev_bias_use = None
                prev_exp = None
                prev_av = None
                prev_evO = None
                def emit_wsl_loads(prx, waittok):
                    tt_ = []
                    for i3 in range(3):
                        p.wait("gpsimd", waittok)
                        tt_.append(p.inc("gpsimd", "dwsl%d" % i3, lambda EE, prx=prx, i3=i3: EE.dma_start(
                            out=wsl[i3][:].rearrange("p (c f) -> p c f", c=8),
                            in_=wqkvT[i3 * 8 + prx].rearrange("c p f -> p c f")), 16))
                    return tt_

                tws_next = emit_wsl_loads(0, None)
                for pr in range(8):
                    tws = tws_next
                    p.wait("gpsimd", prev_bias_use)
                    tbB = []
                    for hh in range(2):
                        h = 2 * pr + hh
                        tbB.append(p.inc("gpsimd", "draw%d" % hh, lambda EE, h=h, hh=hh: EE.dma_start(
                            out=biasB[:, hh * 4480:(hh + 1) * 4480], in_=cbias[h]), 16))
                    texpB = None
                    ev2 = [None, None]
                    for part in range(3):
                        for t8 in range(NQ):
                            qb = (part * NQ + t8) % 2
                            p.wait("tensor", tws[part])
                            p.wait("tensor", ev2[qb])
                            for c in range(NCH):
                                p.inc("tensor", "sP", lambda EE, part=part, t8=t8, c=c, qb=qb: EE.matmul(
                                    pS[0][qb][:],
                                    wsl[part][:].rearrange("p (c f) -> p c f", c=8)[:, c, :],
                                    u_ch[:, c, t8 * 512:(t8 + 1) * 512],
                                    start=(c == 0), stop=(c == NCH - 1)))
                            tmm = p.tok("sP")
                            p.wait("scalar", tmm)
                            if part == 0:
                                tev = p.inc("scalar", "sA", lambda EE, t8=t8, pr=pr, qb=qb: EE.activation(
                                    QT[:, t8 * 512:(t8 + 1) * 512], pS[0][qb][:], AF.Identity,
                                    bias=wbTq[:, pr:pr + 1], scale=1.0 / 32.0))
                            elif part == 1:
                                tev = p.inc("scalar", "sA", lambda EE, t8=t8, pr=pr, qb=qb: EE.activation(
                                    KT[:, t8 * 512:(t8 + 1) * 512], pS[0][qb][:], AF.Identity,
                                    bias=wbTs[:, 8 + pr:9 + pr]))
                            else:
                                tev = p.inc("scalar", "sA", lambda EE, t8=t8, pr=pr, qb=qb: EE.activation(
                                    vts[:, t8 * 512:(t8 + 1) * 512], pS[0][qb][:], AF.Identity,
                                    bias=wbTs[:, 16 + pr:17 + pr]))
                            ev2[qb] = tev
                    evfree = ev2[(3 * NQ - 1) % 2]
                    tVT = p.tok("sA")
                    vgtok = None
                    for g in range(4):
                        p.wait("tensor", tVT)
                        p.wait("tensor", vgtok)
                        for j in range(8):
                            kt = g * 8 + j
                            p.inc("tensor", "sP", lambda EE, kt=kt, j=j: EE.transpose(
                                pvt[:, j * 128:(j + 1) * 128],
                                vts[:, kt * 128:(kt + 1) * 128], idb[:]))
                        ttr = p.tok("sP")
                        p.wait("vector", ttr)
                        pvt3 = pvt[:].rearrange("p (j f) -> p j f", j=8)
                        p.inc("vector", "sV", lambda EE, g=g, pvt3=pvt3: EE.tensor_copy(
                            Vt3[:, g * 8:(g + 1) * 8, 0:64], pvt3[:, :, 0:64]))
                        vgtok = p.inc("vector", "sV", lambda EE, g=g, pvt3=pvt3: EE.tensor_copy(
                            Vt3[:, g * 8:(g + 1) * 8, 65:129], pvt3[:, :, 64:128]))
                    tVdone = p.tok("sV")
                    if pr < 7:
                        tws_next = emit_wsl_loads(pr + 1, p.tok("sP"))
                    p.wait("scalar", tVdone)
                    tqs = p.inc("scalar", "sA", lambda EE: EE.activation(
                        vts[:], QT[:], AF.Square))
                    mbank = [pbc, pS[1][0]]
                    red2 = [None, None]
                    it = 0
                    for hh in range(2):
                        for qt in range(NQ):
                            mb = it % 2
                            p.wait("tensor", tqs)
                            p.wait("tensor", red2[mb])
                            tmmq = p.inc("tensor", "sP", lambda EE, hh=hh, qt=qt, mb=mb: EE.matmul(
                                mbank[mb][:], ones_bf[hh * 64:(hh + 1) * 64, :],
                                vts[hh * 64:(hh + 1) * 64, qt * 512:(qt + 1) * 512],
                                start=True, stop=True, tile_position=(hh * 64, 0)))
                            p.wait("vector", tmmq)
                            red2[mb] = p.inc("vector", "sV", lambda EE, hh=hh, qt=qt, mb=mb: EE.reduce_max(
                                qmax[:, qt * 2 + hh:qt * 2 + hh + 1], mbank[mb][:], axis=AX.X))
                            it += 1
                    p.wait("scalar", red2[0])
                    p.wait("scalar", red2[1])
                    tks_ = p.inc("scalar", "sA", lambda EE: EE.activation(
                        vts[:], KT[:], AF.Square))
                    for hh in range(2):
                        for kb in range(NQ):
                            mb = it % 2
                            p.wait("tensor", tks_)
                            p.wait("tensor", red2[mb])
                            tmmk = p.inc("tensor", "sP", lambda EE, hh=hh, kb=kb, mb=mb: EE.matmul(
                                mbank[mb][:], ones_bf[hh * 64:(hh + 1) * 64, :],
                                vts[hh * 64:(hh + 1) * 64, kb * 512:(kb + 1) * 512],
                                start=True, stop=True, tile_position=(hh * 64, 0)))
                            p.wait("vector", tmmk)
                            tred = p.inc("vector", "sV", lambda EE, mb=mb: EE.reduce_max(
                                kmtmp[:], mbank[mb][:], axis=AX.X))
                            p.wait("vector", tred)
                            if kb == 0:
                                red2[mb] = p.inc("vector", "sV", lambda EE, hh=hh: EE.tensor_copy(
                                    kmax[:, hh:hh + 1], kmtmp[:]))
                            else:
                                red2[mb] = p.inc("vector", "sV", lambda EE, hh=hh: EE.tensor_tensor(
                                    kmax[:, hh:hh + 1], kmax[:, hh:hh + 1], kmtmp[:], ALU.max))
                            it += 1
                    qm3 = qmax[:].rearrange("p (a two) -> p a two", two=2)
                    mp3 = mprod[:].rearrange("p (a two) -> p a two", two=2)
                    p.wait("vector", red2[0])
                    p.wait("vector", red2[1])
                    for hh in range(2):
                        p.inc("vector", "sV", lambda EE, hh=hh, qm3=qm3, mp3=mp3: EE.tensor_scalar(
                            mp3[:, :, hh:hh + 1], qm3[:, :, hh:hh + 1],
                            kmax[:, hh:hh + 1], None, ALU.mult, ALU.bypass))
                    tmp_ = p.tok("sV")
                    p.wait("gpsimd", tmp_)
                    tpw = p.inc("gpsimd", "sG", lambda EE: EE.tensor_tensor(
                        mprod[:], mprod[:], ph16[:], ALU.pow))
                    p.wait("vector", tpw)
                    tnM = p.inc("vector", "sV", lambda EE: EE.tensor_scalar(
                        negM[:], mprod[:], -1.0, None, ALU.mult, ALU.bypass))
                    p.wait("scalar", tnM)
                    tMprep = red2[1] if it % 2 == 0 else red2[0]

                    steps = [(qt, kt) for qt in range(NQ) for kt in range(4 * qt + 4)]
                    NSTP = len(steps)
                    exp_i = [None] * NSTP
                    av_i = [None] * NSTP
                    ev_q = {-1: prev_evO}

                    def emit_av(j):
                        qtj, ktj = steps[j]
                        nktj = 4 * qtj + 4
                        oqj = max(0, (ktj - 4 * qtj) * 128)
                        p.wait("tensor", exp_i[j])
                        if ktj == 0:
                            p.wait("tensor", ev_q[qtj - 1])
                        for hh in range(2):
                            p.inc("tensor", "sP", lambda EE, hh=hh, ktj=ktj, bkj=ktj % 2,
                                  nktj=nktj, oqj=oqj: EE.matmul(
                                pO[hh][0:65, oqj:512], Vt3[:, ktj, hh * 65:(hh + 1) * 65],
                                Pt[hh][bkj][:, oqj:512],
                                start=(ktj == 0), stop=(ktj == nktj - 1)))
                        av_i[j] = p.tok("sP")
                        if ktj == nktj - 1:
                            emit_den(qtj, av_i[j])

                    def emit_den(qtd, tav):
                        p.wait("vector", tav)
                        tr1 = [None, None]
                        for hh in range(2):
                            t_r = p.inc("vector", "sV", lambda EE, hh=hh: EE.reciprocal(
                                rec_f[:, hh * 512:(hh + 1) * 512], pO[hh][64:65, :]))
                            p.wait("vector", t_r)
                            tr1[hh] = p.inc("vector", "sV", lambda EE, hh=hh: EE.tensor_copy(
                                rec_b[:, hh * 512:(hh + 1) * 512],
                                rec_f[:, hh * 512:(hh + 1) * 512]))
                        for hh in range(2):
                            p.wait("tensor", tr1[hh])
                            tr2 = p.inc("tensor", "sP", lambda EE, hh=hh: EE.matmul(
                                pbc[:], ones_bf[0:1, :], rec_b[:, hh * 512:(hh + 1) * 512],
                                start=True, stop=True))
                            p.wait("vector", tr2)
                            tr3 = p.inc("vector", "sV", lambda EE: EE.tensor_copy(
                                recbc[:], pbc[:]))
                            p.wait("vector", tr3)
                            tr4 = p.inc("vector", "sV", lambda EE, hh=hh: EE.tensor_tensor(
                                Ostg[hh * 64:(hh + 1) * 64, :], pO[hh][0:64, :],
                                recbc[0:64, :], ALU.mult))
                            p.wait("tensor", tr4)
                        ev_q[qtd] = p.tok("sV")
                        p.wait("gpsimd", ev_q[qtd])
                        tos = p.inc("gpsimd", "dos", lambda EE, pr=pr, qtd=qtd: EE.dma_start(
                            out=osp[pr, 4 * qtd:4 * qtd + 4].rearrange("t p f -> p t f"),
                            in_=Ostg[:].rearrange("p (t f) -> p t f", t=4)), 16)
                        p.wait("vector", tos)

                    for i, (qt, kt) in enumerate(steps):
                        bk = kt % 2
                        oq = max(0, (kt - 4 * qt) * 128)
                        p.wait("tensor", exp_i[i - 2] if i >= 2 else prev_exp)
                        if i < 2:
                            p.wait("tensor", tMprep)
                            p.wait("tensor", tbB[0])
                            p.wait("tensor", tbB[1])
                        for hh in range(2):
                            p.inc("tensor", "sP", lambda EE, hh=hh, qt=qt, kt=kt, bk=bk, oq=oq: EE.matmul(
                                pS[hh][bk][:, oq:512],
                                KT[hh * 64:(hh + 1) * 64, kt * 128:(kt + 1) * 128],
                                QT[hh * 64:(hh + 1) * 64, qt * 512 + oq:(qt + 1) * 512],
                                start=True, stop=False, tile_position=(hh * 64, 0)))
                        for hh in range(2):
                            off = hh * 4480 + (4 * qt - kt + 3) * 128
                            p.inc("tensor", "sP", lambda EE, hh=hh, bk=bk, off=off, oq=oq: EE.matmul(
                                pS[hh][bk][:, oq:512], idb[:], biasB[:, off + oq:off + 512],
                                start=False, stop=True))
                        tsc = p.tok("sP")
                        if i >= 1:
                            emit_av(i - 1)
                        p.wait("scalar", tsc)
                        p.wait("scalar", av_i[i - 2] if i >= 2 else prev_av)
                        for hh in range(2):
                            p.inc("scalar", "sA", lambda EE, hh=hh, qt=qt, bk=bk, oq=oq: EE.activation(
                                Pt[hh][bk][:, oq:512], pS[hh][bk][:, oq:512], AF.Exp,
                                bias=negM[:, qt * 2 + hh:qt * 2 + hh + 1]))
                        exp_i[i] = p.tok("sA")
                    emit_av(NSTP - 1)
                    prev_bias_use = p.tok("sP")
                    prev_exp = exp_i[-1]
                    prev_av = av_i[-1]
                    prev_evO = ev_q[NQ - 1]
                    wfree = p.tok("sP")
                p.barrier()

            with contextlib.ExitStack() as ps:
                projTs = sbIn(ps, "projTs", [128, 8 * D], BF16)
                zsbp = [sbIn(ps, "zsbp0", [128, D], F32), sbIn(ps, "zsbp1", [128, D], F32)]
                zsbp_free = [None, None]
                Otl = [sbIn(ps, "Otl0", [128, 8 * 128], BF16),
                       sbIn(ps, "Otl1", [128, 8 * 128], BF16)]
                ppj = ps.enter_context(nc.psum_tensor("ppj", [128, 1024], F32))
                tpj = p.inc("gpsimd", "dpj", lambda EE: EE.dma_start(
                    out=projTs[:].rearrange("p (c d_) -> p c d_", c=NCH),
                    in_=projT.rearrange("c p d_ -> p c d_")), 16)
                tpl = [tpj] * NCH
                otok = [None, None]
                pjfree = None
                for tt in range(NT):
                    b = tt % 2
                    p.wait("sync", otok[b])
                    tol = p.inc("sync", "dot%d" % b, lambda EE, tt=tt, b=b: EE.dma_start(
                        out=Otl[b][:].rearrange("p (c f) -> p c f", c=8),
                        in_=osp[:, tt].rearrange("c p f -> p c f")), 16)
                    p.wait("tensor", tol)
                    p.wait("tensor", pjfree)
                    for c in range(NCH):
                        p.wait("tensor", tpl[c])
                        for dh in range(2):
                            p.inc("tensor", "sP", lambda EE, c=c, dh=dh, b=b: EE.matmul(
                                ppj[:, dh * 512:(dh + 1) * 512],
                                Otl[b][:, c * 128:(c + 1) * 128],
                                projTs[:, c * D + dh * 512:c * D + (dh + 1) * 512],
                                start=(c == 0), stop=False))
                    ui = tt % 4
                    p.wait("sync", ep_state["utk_free"][ui])
                    tul = p.inc("sync", "dut%d" % ui, lambda EE, tt=tt, ui=ui: EE.dma_start(
                        out=utk[ui][:], in_=utsp[tt * 128:(tt + 1) * 128, :]), 16)
                    p.wait("tensor", tul)
                    for dh in range(2):
                        p.inc("tensor", "sP", lambda EE, dh=dh, ui=ui: EE.matmul(
                            ppj[:, dh * 512:(dh + 1) * 512], idb[:],
                            utk[ui][:, dh * 512:(dh + 1) * 512], start=False, stop=False))
                        p.inc("tensor", "sP", lambda EE, dh=dh: EE.matmul(
                            ppj[:, dh * 512:(dh + 1) * 512], idb[:],
                            pjbc[:, dh * 512:(dh + 1) * 512], start=False, stop=True))
                    tz = p.tok("sP")
                    ep_state["utk_free"][ui] = tz
                    otok[b] = tz
                    p.wait("vector", tz)
                    p.wait("vector", zsbp_free[b])
                    tevz = p.inc("vector", "sV", lambda EE, b=b: EE.tensor_copy(
                        zsbp[b][:], ppj[:]))
                    pjfree = tevz
                    zsbp_free[b] = emit_epilogue(zsbp[b], tevz, x1sp, x2sp, tt,
                                                 lgbc, lbbc, False)
            p.barrier()

        ffn_call(x0, x1sp, False, "F1", post_stats=emit_prepass)
        attn_call()
        ffn_call(x2sp, out, True, "F2")
        p.wait("gpsimd", p.tok("dxo0"))
        p.wait("gpsimd", p.tok("dxo1"))

        block = es.enter_context(nc.Block())

        @block.sync
        def _(EE):
            for fn in p.q["sync"]:
                fn(EE)

        @block.gpsimd
        def _(EE):
            for fn in p.q["gpsimd"]:
                fn(EE)

        @block.tensor
        def _(EE):
            for fn in p.q["tensor"]:
                fn(EE)

        @block.vector
        def _(EE):
            for fn in p.q["vector"]:
                fn(EE)

        @block.scalar
        def _(EE):
            for fn in p.q["scalar"]:
                fn(EE)

    return nc


def host_layouts(ins):
    W_w = np.ascontiguousarray(ins["L0_attn_W_w"], np.float32)
    wqkvT = np.ascontiguousarray(W_w.reshape(24, 128, 8, 128).transpose(0, 2, 3, 1))
    wbT = np.ascontiguousarray(np.asarray(ins["L0_attn_W_b"], np.float32).reshape(24, 128).T)
    proj_w = np.asarray(ins["L0_attn_proj_w"], np.float32)
    projT = np.ascontiguousarray(proj_w.reshape(1024, 8, 128).transpose(1, 2, 0))
    rpb = np.asarray(ins["L0_attn_rpb"], np.float32)
    kk = np.arange(128)[:, None, None]
    bb = np.arange(35)[None, :, None]
    qq = np.arange(128)[None, None, :]
    midx = kk - qq - 128 * (bb - 3) + 4095
    valid = midx <= 4095
    mclip = np.clip(midx, 0, 4094)
    cb = rpb[mclip]
    cb = np.where(valid[..., None], cb, np.float32(-1e30))
    cbias = np.ascontiguousarray(cb.transpose(3, 0, 1, 2).reshape(16, 128, 35 * 128))
    wg = np.ascontiguousarray(np.asarray(ins["L0_ffn_wg"], np.float32).reshape(8, 128, 16))
    w1 = np.ascontiguousarray(np.asarray(ins["L0_ffn_w1"], np.float32).reshape(16, 8, 128, 256))
    b1T = np.ascontiguousarray(np.asarray(ins["L0_ffn_b1"], np.float32)
                               .reshape(16, 2, 128).transpose(2, 0, 1).reshape(128, 32))
    w2 = np.ascontiguousarray(np.asarray(ins["L0_ffn_w2"], np.float32).reshape(16, 2, 128, 1024))
    c_idf = np.eye(128, dtype=np.float32)
    r = np.arange(128)[:, None]
    q = np.arange(512)[None, :]
    c_tri = np.zeros((5, 128, 512), np.float32)
    for j in range(4):
        c_tri[j] = (128 * j + r <= q)
    c_tri[4] = 1.0
    c_e16 = np.zeros((16, 2048), np.float32)
    for e in range(16):
        c_e16[e, e * 128:(e + 1) * 128] = 1.0
    com = {
        "wqkvT": wqkvT, "wbT": wbT, "projT": projT,
        "projb": np.asarray(ins["L0_attn_proj_b"], np.float32).reshape(1, 1024),
        "lng": np.asarray(ins["L0_attn_ln_g"], np.float32).reshape(1, 1024),
        "lnb": np.asarray(ins["L0_attn_ln_b"], np.float32).reshape(1, 1024),
        "cbias": cbias, "wg": wg,
        "bg": np.asarray(ins["L0_ffn_bg"], np.float32).reshape(1, 16),
        "w1": w1, "b1T": b1T, "w2": w2,
        "b2": np.asarray(ins["L0_ffn_b2"], np.float32),
        "gam": np.asarray(ins["L0_ffn_gamma"], np.float32).reshape(1, 1024),
        "bet": np.asarray(ins["L0_ffn_beta"], np.float32).reshape(1, 1024),
        "c_idf": c_idf, "c_tri": c_tri, "c_e16": c_e16,
    }
    return {k: np.ascontiguousarray(v, dtype=np.float32) for k, v in com.items()}


def kernel(**inputs):
    com = host_layouts(inputs)
    x = np.asarray(inputs["L0_attn_x"], np.float32)
    nc = build_nc()
    in_maps = []
    for b in range(8):
        m = dict(com)
        m["x0"] = np.ascontiguousarray(x[b])
        in_maps.append(m)
    res = run_bass_kernel_spmd(nc, in_maps, core_ids=list(range(8)))
    return np.stack([r["out"] for r in res.results], axis=0)


# version 13 (final)
# speedup vs baseline: 1.0426x; 1.0426x over previous
import contextlib
import numpy as np
import concourse.bass as bass
import concourse.mybir as mybir
from concourse.bass_utils import run_bass_kernel_spmd

F32 = mybir.dt.float32
BF16 = mybir.dt.bfloat16
AF = mybir.ActivationFunctionType
ALU = mybir.AluOpType
AX = mybir.AxisListType

S, D, H, HD, E, FP = 4096, 1024, 16, 64, 16, 256
NT = S // 128
NQ = S // 512
NCH = D // 128
NST = S // 512
EPS = 1e-5

ENGINES = ("sync", "gpsimd", "tensor", "vector", "scalar")


class Prog:
    def __init__(self, sems):
        self.q = {e: [] for e in ENGINES}
        self.sems = sems
        self.cnt = {n: 0 for n in sems}
        self.lastwait = {e: {n: 0 for n in sems} for e in ENGINES}

    def inc(self, eng, sem, fn, by=1):
        self.cnt[sem] += by
        val = self.cnt[sem]
        s = self.sems[sem]
        self.q[eng].append(lambda EE, fn=fn, s=s, by=by: fn(EE).then_inc(s, by))
        return (sem, val)

    def emit(self, eng, fn):
        self.q[eng].append(fn)

    def tok(self, sem):
        return (sem, self.cnt[sem])

    def wait(self, eng, tok):
        if tok is None:
            return
        sem, val = tok
        if self.lastwait[eng][sem] >= val:
            return
        self.lastwait[eng][sem] = val
        s = self.sems[sem]
        self.q[eng].append(lambda EE, s=s, v=val: EE.wait_ge(s, v))

    def barrier(self):
        for d in self.dma_sems:
            self.wait("sync", self.tok(d))
            self.wait("gpsimd", self.tok(d))
        toks = [self.inc(e, "bar", lambda EE: EE.nop()) for e in ENGINES]
        for e in ENGINES:
            for t in toks:
                self.wait(e, t)


def build_nc():
    nc = bass.Bass("TRN2", target_bir_lowering=False)
    dtn = nc.dram_tensor
    x0 = dtn("x0", [S, D], F32, kind="ExternalInput").ap()
    wqkvT = dtn("wqkvT", [24, 8, 128, 128], F32, kind="ExternalInput").ap()
    wbT = dtn("wbT", [128, 24], F32, kind="ExternalInput").ap()
    projT = dtn("projT", [8, 128, D], F32, kind="ExternalInput").ap()
    projb = dtn("projb", [1, D], F32, kind="ExternalInput").ap()
    lng = dtn("lng", [1, D], F32, kind="ExternalInput").ap()
    lnb = dtn("lnb", [1, D], F32, kind="ExternalInput").ap()
    cbias = dtn("cbias", [H, 128, 4480], F32, kind="ExternalInput").ap()
    wg_in = dtn("wg", [8, 128, E], F32, kind="ExternalInput").ap()
    bg_in = dtn("bg", [1, E], F32, kind="ExternalInput").ap()
    w1_in = dtn("w1", [E, 8, 128, FP], F32, kind="ExternalInput").ap()
    b1T_in = dtn("b1T", [128, 32], F32, kind="ExternalInput").ap()
    w2_in = dtn("w2", [E, 2, 128, D], F32, kind="ExternalInput").ap()
    b2_in = dtn("b2", [E, D], F32, kind="ExternalInput").ap()
    gam = dtn("gam", [1, D], F32, kind="ExternalInput").ap()
    bet = dtn("bet", [1, D], F32, kind="ExternalInput").ap()
    c_idf = dtn("c_idf", [128, 128], F32, kind="ExternalInput").ap()
    c_tri = dtn("c_tri", [5, 128, 512], F32, kind="ExternalInput").ap()
    c_e16 = dtn("c_e16", [16, 2048], F32, kind="ExternalInput").ap()
    out = dtn("out", [S, D], F32, kind="ExternalOutput").ap()
    x1sp = dtn("x1sp", [S, D], F32, kind="Internal").ap()
    x2sp = dtn("x2sp", [S, D], F32, kind="Internal").ap()
    utsp = dtn("utsp", [S, D], BF16, kind="Internal").ap()
    w1bf = dtn("w1bf", [E, 8, 128, FP], BF16, kind="Internal").ap()
    w2bf = dtn("w2bf", [E, 2, 128, D], BF16, kind="Internal").ap()
    osp = dtn("osp", [NCH, NT, 128, 128], BF16, kind="Internal").ap()

    es = contextlib.ExitStack()
    with es:
        def sbIn(stack, nm, shp, dtp=F32):
            return stack.enter_context(nc.sbuf_tensor(nm, shp, dtp))

        sb = lambda nm, shp, dtp=F32: sbIn(es, nm, shp, dtp)
        sems = {}
        DMA_SEMS = (["dPreS", "dPreG", "dos", "dpj"]
                    + ["dWp%d" % i for i in range(4)]
                    + ["dxa%d" % i for i in range(4)]
                    + ["dup%d" % i for i in range(2)]
                    + ["dw1%d" % i for i in range(2)]
                    + ["dw2%d" % i for i in range(2)]
                    + ["dut%d" % i for i in range(4)]
                    + ["dxp%d" % i for i in range(4)]
                    + ["dxo%d" % i for i in range(2)]
                    + ["dwsl%d" % i for i in range(3)]
                    + ["draw%d" % i for i in range(2)]
                    + ["dot%d" % i for i in range(2)])
        for n in ["sP", "sV", "sA", "sG", "bar"] + DMA_SEMS:
            sems[n] = es.enter_context(nc.semaphore(n))
        p = Prog(sems)
        p.dma_sems = DMA_SEMS

        u_sb = sb("u_sb", [128, NCH * S], BF16)
        u_ch = u_sb[:].rearrange("p (c s) -> p c s", c=NCH)
        idf = sb("idf", [128, 128], F32)
        idb = sb("idb", [128, 128], BF16)
        ones_bf = sb("ones_bf", [128, 128], BF16)
        ones1f = sb("ones1f", [1, 128], F32)
        epst = sb("epst", [128, 1], F32)
        mhalf = sb("mhalf", [128, 1], F32)
        mh_nt = sb("mh_nt", [128, NT], F32)
        ph16 = sb("ph16", [128, 16], F32)
        b1Ts = sb("b1Ts", [128, 32], F32)
        wbTs = sb("wbTs", [128, 24], F32)
        wbTq = sb("wbTq", [128, 8], F32)
        b2sb = sb("b2sb", [16, D], BF16)
        wgsb = sb("wgsb", [128, 8 * E], F32)
        wg3 = wgsb[:].rearrange("p (c e) -> p c e", c=8)
        bgsb = sb("bgsb", [1, E], F32)
        gbc = sb("gbc", [128, D], BF16)
        bbc = sb("bbc", [128, D], BF16)
        lgbc = sb("lgbc", [128, D], BF16)
        lbbc = sb("lbbc", [128, D], BF16)
        pjbc = sb("pjbc", [128, D], BF16)
        selmT = sb("selmT", [16, S], BF16)
        R_all = sb("R_all", [128, NT * E], BF16)
        Z_all = sb("Z_all", [128, NT * E], BF16)
        mv_all = sb("mv_all", [128, 2 * NT], F32)
        mv3 = mv_all[:].rearrange("p (t two) -> p t two", two=2)
        rstd_a = sb("rstd_a", [128, NT], F32)
        nmur_a = sb("nmur_a", [128, NT], F32)
        ve_a = sb("ve_a", [128, NT], F32)
        xp = [sb("xp%d" % i, [128, D], F32) for i in range(4)]
        xo = [sb("xo0", [128, D], F32), sb("xo1", [128, D], F32)]
        utk = [sb("utk%d" % i, [128, D], BF16) for i in range(4)]
        xhat = [sb("xhat0", [128, D], BF16), sb("xhat1", [128, D], BF16)]
        st12 = sb("st12", [128, 12], F32)
        mv2 = sb("mv2", [128, 2], F32)
        ve2 = sb("ve2", [128, 1], F32)
        rs2 = sb("rs2", [128, 1], F32)
        nm2 = sb("nm2", [128, 1], F32)
        sqv = sb("sqv", [128, 1], F32)
        lg2 = sb("lg2", [128, E], F32)
        lgt = sb("lgt", [128, E], F32)
        m1t = sb("m1t", [128, 1], F32)
        m2t = sb("m2t", [128, 1], F32)
        pen = sb("pen", [128, E], F32)
        ne_sb = sb("ne_sb", [1, E], F32)
        fill = sb("fill", [E, 1], F32)
        cmp_t = sb("cmp_t", [E, 512], F32)

        pre = []

        def preS(fn):
            p.wait("sync", p.tok("dPreS"))
            pre.append(p.inc("sync", "dPreS", fn, 16))

        def preG(fn):
            p.wait("gpsimd", p.tok("dPreG"))
            pre.append(p.inc("gpsimd", "dPreG", fn, 16))

        preS(lambda EE: EE.dma_start(out=idf[:], in_=c_idf))
        preG(lambda EE: EE.dma_start(out=idb[:], in_=c_idf))
        preS(lambda EE: EE.dma_start(out=b1Ts[:], in_=b1T_in))
        preS(lambda EE: EE.dma_start(out=wbTs[:], in_=wbT))
        preG(lambda EE: EE.dma_start(out=b2sb[:], in_=b2_in))
        preS(lambda EE: EE.dma_start(out=wg3, in_=wg_in.rearrange("c p e -> p c e")))
        preS(lambda EE: EE.dma_start(out=bgsb[:], in_=bg_in))
        for tk in pre:
            for e_ in ENGINES:
                p.wait(e_, tk)
        p.emit("gpsimd", lambda EE: EE.memset(ones_bf[:], 1.0))
        p.emit("gpsimd", lambda EE: EE.memset(ones1f[:], 1.0))
        p.emit("gpsimd", lambda EE: EE.memset(epst[:], EPS))
        p.emit("gpsimd", lambda EE: EE.memset(mhalf[:], -0.5))
        p.emit("gpsimd", lambda EE: EE.memset(mh_nt[:], -0.5))
        tG0 = p.inc("gpsimd", "sG", lambda EE: EE.memset(ph16[:], 0.5))
        for e_ in ENGINES:
            p.wait(e_, tG0)
        tV0 = p.inc("vector", "sV", lambda EE: EE.tensor_scalar(
            wbTq[:], wbTs[:, 0:8], 1.0 / 32.0, None, ALU.mult, ALU.bypass))
        p.wait("scalar", tV0)

        with contextlib.ExitStack() as ps0:
            pbc0 = ps0.enter_context(nc.psum_tensor("pbc0", [128, 512], F32))
            rowbuf = sbIn(ps0, "rowbuf", [1, 5 * D], F32)
            rbt = []
            for i, src in enumerate([gam, bet, lng, lnb, projb]):
                p.wait("sync", p.tok("dPreS"))
                rbt.append(p.inc("sync", "dPreS", lambda EE, i=i, src=src: EE.dma_start(
                    out=rowbuf[:, i * D:(i + 1) * D], in_=src), 16))
            for i, (dst, sc) in enumerate([(gbc, 0.5), (bbc, 0.5), (lgbc, 1.0),
                                           (lbbc, 1.0), (pjbc, 1.0)]):
                p.wait("tensor", rbt[i])
                for h2 in range(2):
                    tmm = p.inc("tensor", "sP", lambda EE, i=i, h2=h2: EE.matmul(
                        pbc0[:], ones1f[:],
                        rowbuf[0:1, i * D + h2 * 512:i * D + (h2 + 1) * 512],
                        start=True, stop=True))
                    p.wait("scalar", tmm)
                    tev = p.inc("scalar", "sA", lambda EE, dst=dst, sc=sc, h2=h2: EE.activation(
                        dst[:, h2 * 512:(h2 + 1) * 512], pbc0[:], AF.Copy, scale=sc))
                    p.wait("tensor", tev)
        p.barrier()
        wst4 = [sb("wst%d" % i, [128, 2048], BF16) for i in range(2)]
        wtoks = {}

        def emit_prepass():
            stok = [None] * 4
            for e in range(E):
                for w_i, (src, dst3) in enumerate([
                        (w1_in[e].rearrange("c p f -> p c f"),
                         w1bf[e].rearrange("c p f -> p c f")),
                        (w2_in[e].rearrange("c p d_ -> p c d_"),
                         w2bf[e].rearrange("c p d_ -> p c d_"))]):
                    sl = (2 * e + w_i) % 2
                    nchp = 8 if w_i == 0 else 2
                    p.wait("gpsimd", stok[sl])
                    tl_ = p.inc("gpsimd", "dWp%d" % sl, lambda EE, src=src, sl=sl, nchp=nchp: EE.dma_start(
                        out=wst4[sl][:].rearrange("p (c f) -> p c f", c=nchp), in_=src), 16)
                    p.wait("gpsimd", tl_)
                    stok[sl] = p.inc("gpsimd", "dWp%d" % sl, lambda EE, dst3=dst3, sl=sl, nchp=nchp: EE.dma_start(
                        out=dst3, in_=wst4[sl][:].rearrange("p (c f) -> p c f", c=nchp)), 16)
                    wtoks[(e, w_i)] = stok[sl]

        def phase1(xin, gate, sfx, xa, ub, uT32, upk):
            with contextlib.ExitStack() as ps:
                ptr = ps.enter_context(nc.psum_tensor("ptr" + sfx, [128, D], F32))
                pgate = (ps.enter_context(nc.psum_tensor("pg" + sfx, [128, E], F32))
                         if gate else None)
                xfree = [None] * 4
                ubfree = [None, None]
                uTfree = [None, None]
                ptrfree = None
                upk_st = [None, None]
                gate_tok = None
                lgt_tok = None
                stat_tok = {}
                for it in range(NT + 2):
                    if it < NT:
                        t = it
                        b = t % 4
                        p.wait("sync", xfree[b])
                        tl = p.inc("sync", "dxa%d" % b, lambda EE, t=t, b=b: EE.dma_start(
                            out=xa[b][:], in_=xin[t * 128:(t + 1) * 128, :]), 16)
                        p.wait("vector", tl)
                        t1 = p.inc("vector", "sV", lambda EE, b=b: EE.bn_stats(
                            st12[:, 0:6], xa[b][:, 0:512]))
                        p.wait("vector", t1)
                        t2 = p.inc("vector", "sV", lambda EE, b=b: EE.bn_stats(
                            st12[:, 6:12], xa[b][:, 512:1024]))
                        p.wait("vector", t2)
                        t3 = p.inc("vector", "sV", lambda EE: EE.bn_aggr(mv2[:], st12[:]))
                        p.wait("vector", t3)
                        t4 = p.inc("vector", "sV", lambda EE: EE.tensor_scalar(
                            ve2[:], mv2[:, 1:2], epst[:], None, ALU.add, ALU.bypass))
                        p.wait("scalar", t4)
                        t5 = p.inc("scalar", "sA", lambda EE: EE.activation(
                            sqv[:], ve2[:], AF.Sqrt))
                        p.wait("vector", t5)
                        t6 = p.inc("vector", "sV", lambda EE, t=t: EE.reciprocal(
                            rstd_a[:, t:t + 1], sqv[:]))
                        p.wait("vector", t6)
                        t7 = p.inc("vector", "sV", lambda EE, t=t: EE.tensor_tensor(
                            nmur_a[:, t:t + 1], mv2[:, 0:1], rstd_a[:, t:t + 1], ALU.mult))
                        p.wait("vector", t7)
                        t8 = p.inc("vector", "sV", lambda EE, t=t: EE.tensor_scalar(
                            nmur_a[:, t:t + 1], nmur_a[:, t:t + 1], -1.0, None,
                            ALU.mult, ALU.bypass))
                        stat_tok[t] = t8
                    if it < 2:
                        continue
                    t = it - 2
                    b = t % 4
                    b2_ = t % 2
                    p.wait("vector", stat_tok[t])
                    p.wait("vector", ubfree[b2_])
                    tu = p.inc("vector", "sV", lambda EE, t=t, b=b, b2_=b2_: EE.tensor_scalar(
                        ub[b2_][:], xa[b][:], rstd_a[:, t:t + 1], nmur_a[:, t:t + 1],
                        ALU.mult, ALU.add))
                    xfree[b] = tu
                    p.wait("sync", upk_st[b2_])
                    p.wait("vector", tu)
                    tuk = p.inc("vector", "sV", lambda EE, b2_=b2_: EE.tensor_copy(
                        upk[b2_][:], ub[b2_][:]))
                    p.wait("sync", tuk)
                    tst_ = p.inc("sync", "dup%d" % b2_, lambda EE, t=t, b2_=b2_: EE.dma_start(
                        out=utsp[t * 128:(t + 1) * 128, :], in_=upk[b2_][:]), 16)
                    upk_st[b2_] = tst_
                    p.wait("tensor", tu)
                    p.wait("tensor", ptrfree)
                    for c in range(NCH):
                        p.inc("tensor", "sP", lambda EE, b2_=b2_, c=c: EE.transpose(
                            ptr[:, c * 128:(c + 1) * 128],
                            ub[b2_][:, c * 128:(c + 1) * 128], idf[:]))
                    ttr = p.tok("sP")
                    ubfree[b2_] = ttr
                    p.wait("scalar", ttr)
                    p.wait("scalar", uTfree[b2_])
                    tev = p.inc("scalar", "sA", lambda EE, b2_=b2_: EE.activation(
                        uT32[b2_][:], ptr[:], AF.Copy))
                    ptrfree = tev
                    p.wait("vector", tev)
                    tub = p.inc("vector", "sV", lambda EE, t=t, b2_=b2_: EE.tensor_copy(
                        u_ch[:, :, t * 128:(t + 1) * 128],
                        uT32[b2_][:].rearrange("p (c f) -> p c f", c=NCH)))
                    uTfree[b2_] = tub
                    if gate:
                        p.wait("tensor", tev)
                        p.wait("tensor", gate_tok)
                        for c in range(NCH):
                            p.inc("tensor", "sP", lambda EE, b2_=b2_, c=c: EE.matmul(
                                pgate[:], uT32[b2_][:, c * 128:(c + 1) * 128],
                                wg3[:, c, :], start=(c == 0), stop=False))
                        tgm = p.inc("tensor", "sP", lambda EE: EE.matmul(
                            pgate[:], ones1f[:], bgsb[:], start=False, stop=True))
                        uTfree[b2_] = tgm
                        p.wait("vector", tgm)
                        p.wait("vector", gate_tok)
                        p.wait("vector", lgt_tok)
                        tcp = p.inc("vector", "sV", lambda EE: EE.tensor_copy(
                            lgt[:], pgate[:]))
                        gate_tok = tcp
                        p.wait("vector", tcp)
                        g1 = p.inc("vector", "sV", lambda EE: EE.reduce_max(
                            m1t[:], lgt[:], axis=AX.X))
                        p.wait("vector", g1)
                        g2 = p.inc("vector", "sV", lambda EE: EE.tensor_scalar(
                            pen[:], lgt[:], m1t[:], -1e30, ALU.is_ge, ALU.mult))
                        p.wait("vector", g2)
                        g3 = p.inc("vector", "sV", lambda EE: EE.tensor_tensor(
                            lg2[:], lgt[:], pen[:], ALU.add))
                        p.wait("vector", g3)
                        g4 = p.inc("vector", "sV", lambda EE: EE.reduce_max(
                            m2t[:], lg2[:], axis=AX.X))
                        p.wait("vector", g4)
                        g5 = p.inc("vector", "sV", lambda EE, t=t: EE.tensor_scalar(
                            R_all[:, t * E:(t + 1) * E], lgt[:], m2t[:], None,
                            ALU.is_ge, ALU.bypass))
                        p.wait("vector", g5)
                        p.inc("vector", "sV", lambda EE, t=t: EE.tensor_scalar(
                            Z_all[:, t * E:(t + 1) * E], R_all[:, t * E:(t + 1) * E],
                            -1.0, 1.0, ALU.mult, ALU.add))
                        lgt_tok = g5

        def routing_finalize(sfx):
            p.wait("tensor", p.tok("sV"))
            p.wait("tensor", p.tok("sG"))
            with contextlib.ExitStack() as ps:
                pne = ps.enter_context(nc.psum_tensor("pne" + sfx, [1, E], F32))
                pne2 = ps.enter_context(nc.psum_tensor("pne2" + sfx, [E, 1], F32))
                pzc = ps.enter_context(nc.psum_tensor("pzc" + sfx, [E, 512], F32))
                pRT = ps.enter_context(nc.psum_tensor("pRT" + sfx, [E, 512], BF16))
                for t in range(NT):
                    p.inc("tensor", "sP", lambda EE, t=t: EE.matmul(
                        pne[:], ones_bf[:, 0:1], R_all[:, t * E:(t + 1) * E],
                        start=(t == 0), stop=(t == NT - 1)))
                tne = p.tok("sP")
                p.wait("vector", tne)
                tnesb = p.inc("vector", "sV", lambda EE: EE.tensor_copy(ne_sb[:], pne[:]))
                p.wait("tensor", tnesb)
                tne2 = p.inc("tensor", "sP", lambda EE: EE.matmul(
                    pne2[:], ne_sb[:], ones1f[0:1, 0:1], start=True, stop=True))
                p.wait("vector", tne2)
                tfill = p.inc("vector", "sV", lambda EE: EE.tensor_scalar(
                    fill[:], pne2[:], -1.0, float(S // 2), ALU.mult, ALU.add))
                prevsel = None
                for tq in range(NQ):
                    nch_ = 4 * tq + 4
                    p.wait("tensor", prevsel)
                    for c in range(nch_):
                        pat = 4 if c < 4 * tq else (c - 4 * tq)
                        p.inc("tensor", "sP", lambda EE, c=c, pat=pat, nch_=nch_: EE.matmul(
                            pzc[:], Z_all[:, c * E:(c + 1) * E],
                            fb["tri5"][:, pat * 512:(pat + 1) * 512],
                            start=(c == 0), stop=(c == nch_ - 1)))
                    tzc = p.tok("sP")
                    for j in range(4):
                        c = 4 * tq + j
                        p.inc("tensor", "sP", lambda EE, c=c, j=j: EE.transpose(
                            pRT[:, j * 128:(j + 1) * 128], R_all[:, c * E:(c + 1) * E],
                            idb[:]))
                    tRT = p.tok("sP")
                    p.wait("vector", tzc)
                    p.wait("vector", tRT)
                    p.wait("vector", tfill)
                    tc = p.inc("vector", "sV", lambda EE: EE.tensor_scalar(
                        cmp_t[:], pzc[:], fill[:], None, ALU.is_le, ALU.bypass))
                    p.wait("vector", tc)
                    tsel = p.inc("vector", "sV", lambda EE, tq=tq: EE.tensor_tensor(
                        selmT[:, tq * 512:(tq + 1) * 512], cmp_t[:], pRT[:], ALU.max))
                    prevsel = tsel

        fb = {}
        ep_state = {"xp_free": [None] * 4, "xo_free": [None, None],
                    "xh_free": [None, None], "utk_free": [None] * 4}

        def emit_epilogue(pzt, tz, xin, xout, tt, g_bc, b_bc, final):
            b = tt % 2
            p.wait("vector", tz)
            t1 = p.inc("vector", "sV", lambda EE, pzt=pzt: EE.bn_stats(
                st12[:, 0:6], pzt[:, 0:512]))
            p.wait("vector", t1)
            t2 = p.inc("vector", "sV", lambda EE, pzt=pzt: EE.bn_stats(
                st12[:, 6:12], pzt[:, 512:1024]))
            p.wait("vector", t2)
            t3 = p.inc("vector", "sV", lambda EE: EE.bn_aggr(mv2[:], st12[:]))
            p.wait("vector", t3)
            t4 = p.inc("vector", "sV", lambda EE: EE.tensor_scalar(
                ve2[:], mv2[:, 1:2], epst[:], None, ALU.add, ALU.bypass))
            p.wait("gpsimd", t4)
            t5 = p.inc("gpsimd", "sG", lambda EE: EE.tensor_tensor(
                rs2[:], ve2[:], mhalf[:], ALU.pow))
            p.wait("vector", t5)
            t6 = p.inc("vector", "sV", lambda EE: EE.tensor_tensor(
                nm2[:], mv2[:, 0:1], rs2[:], ALU.mult))
            p.wait("vector", t6)
            t7 = p.inc("vector", "sV", lambda EE: EE.tensor_scalar(
                nm2[:], nm2[:], -1.0, None, ALU.mult, ALU.bypass))
            p.wait("vector", t7)
            p.wait("vector", ep_state["xh_free"][b])
            t8 = p.inc("vector", "sV", lambda EE, pzt=pzt, b=b: EE.tensor_scalar(
                xhat[b][:], pzt[:], rs2[:], nm2[:], ALU.mult, ALU.add))
            b4 = tt % 4
            p.wait("sync", ep_state["xp_free"][b4])
            tl = p.inc("sync", "dxp%d" % b4, lambda EE, tt=tt, b4=b4, xin=xin: EE.dma_start(
                out=xp[b4][:], in_=xin[tt * 128:(tt + 1) * 128, :]), 16)
            p.wait("vector", t8)
            t9 = p.inc("vector", "sV", lambda EE, b=b, g_bc=g_bc: EE.tensor_tensor(
                xhat[b][:], xhat[b][:], g_bc[:], ALU.mult))
            p.wait("vector", t9)
            t10 = p.inc("vector", "sV", lambda EE, b=b, b_bc=b_bc: EE.tensor_tensor(
                xhat[b][:], xhat[b][:], b_bc[:], ALU.add))
            p.wait("vector", tl)
            p.wait("vector", t10)
            p.wait("vector", ep_state["xo_free"][b])
            t11 = p.inc("vector", "sV", lambda EE, b=b, b4=b4: EE.tensor_tensor(
                xo[b][:], xp[b4][:], xhat[b][:], ALU.add))
            ep_state["xp_free"][b4] = t11
            ep_state["xh_free"][b] = t11
            p.wait("gpsimd", t11)
            tst = p.inc("gpsimd", "dxo%d" % b, lambda EE, tt=tt, b=b, xout=xout: EE.dma_start(
                out=xout[tt * 128:(tt + 1) * 128, :], in_=xo[b][:]), 16)
            ep_state["xo_free"][b] = tst
            return t8

        def ffn_phase2(xin, xout, final, sfx):
            with contextlib.ExitStack() as ps:
                w1s = [sbIn(ps, "w1s0" + sfx, [128, 8 * FP], BF16),
                       sbIn(ps, "w1s1" + sfx, [128, 8 * FP], BF16)]
                w2s = [sbIn(ps, "w2s0" + sfx, [128, 2 * D], BF16),
                       sbIn(ps, "w2s1" + sfx, [128, 2 * D], BF16)]
                Hm = sbIn(ps, "Hm" + sfx, [128, E * 512], BF16)
                Hm2 = sbIn(ps, "Hm2" + sfx, [128, E * 512], BF16)
                mkbc = sbIn(ps, "mkbc" + sfx, [128, 512], BF16)
                zsb = [sbIn(ps, "zsb0" + sfx, [128, D], F32),
                       sbIn(ps, "zsb1" + sfx, [128, D], F32)]
                zsb_free = [None, None]
                pz = [ps.enter_context(nc.psum_tensor("pz%d" % i + sfx, [128, 1024], F32))
                      for i in range(4)]
                w1ld = [None, None]
                w2ld = [None, None]
                ep_free = [None, None, None, None]
                gel_last = None
                mk_last = None
                w2_last = None
                for st in range(NST):
                    s0 = st * 512
                    gel_e = {}
                    for e in range(E):
                        wb_ = e % 2
                        if e == 0:
                            p.wait("tensor", ep_free[0])
                            p.wait("tensor", ep_free[1])
                        if e == 1:
                            p.wait("tensor", ep_free[2])
                            p.wait("tensor", ep_free[3])
                        p.wait("sync", wtoks[(e, 0)])
                        p.wait("sync", w1ld[wb_])
                        tw1 = p.inc("sync", "dw1%d" % wb_, lambda EE, e=e, wb_=wb_: EE.dma_start(
                            out=w1s[wb_][:].rearrange("p (c f) -> p c f", c=8),
                            in_=w1bf[e].rearrange("c p f -> p c f")), 16)
                        p.wait("tensor", tw1)
                        p.wait("tensor", gel_e.get(e - 2))
                        pza, pzb = (pz[0], pz[1]) if e % 2 == 0 else (pz[2], pz[3])
                        for f in range(2):
                            ptgt = pza if f == 0 else pzb
                            for c in range(NCH):
                                p.inc("tensor", "sP", lambda EE, wb_=wb_, f=f, c=c, s0=s0,
                                      ptgt=ptgt: EE.matmul(
                                    ptgt[:, 0:512],
                                    w1s[wb_][:].rearrange("p (c g) -> p c g", c=8)[:, c, f * 128:(f + 1) * 128],
                                    u_ch[:, c, s0:s0 + 512],
                                    start=(c == 0), stop=(c == NCH - 1)))
                        tw1d = p.tok("sP")
                        w1ld[wb_] = tw1d
                        p.wait("tensor", mk_last)
                        tmk = p.inc("tensor", "sP", lambda EE, e=e, s0=s0: EE.matmul(
                            pz[1][:, 512:1024], fb["e16"][:, e * 128:(e + 1) * 128],
                            selmT[:, s0:s0 + 512], start=True, stop=True))
                        p.wait("scalar", tw1d)
                        p.wait("scalar", w2_last)
                        for f in range(2):
                            dst = Hm if f == 0 else Hm2
                            ptgt = pza if f == 0 else pzb
                            p.inc("scalar", "sA", lambda EE, e=e, f=f, dst=dst, ptgt=ptgt: EE.activation(
                                dst[:, e * 512:(e + 1) * 512], ptgt[:, 0:512], AF.Gelu,
                                bias=b1Ts[:, e * 2 + f:e * 2 + f + 1]))
                        gel_last = p.tok("sA")
                        gel_e[e] = gel_last
                        p.wait("vector", tmk)
                        p.wait("vector", mk_last)
                        tmc = p.inc("vector", "sV", lambda EE: EE.tensor_copy(
                            mkbc[:], pz[1][:, 512:1024]))
                        mk_last = tmc
                        p.wait("vector", gel_last)
                        for f in range(2):
                            dst = Hm if f == 0 else Hm2
                            p.inc("vector", "sV", lambda EE, e=e, dst=dst: EE.tensor_tensor(
                                dst[:, e * 512:(e + 1) * 512],
                                dst[:, e * 512:(e + 1) * 512], mkbc[:], ALU.mult))
                        hm_all = p.tok("sV")
                    for e in range(E):
                        wb_ = e % 2
                        p.wait("sync", wtoks[(e, 1)])
                        p.wait("sync", w2ld[wb_])
                        tw2 = p.inc("sync", "dw2%d" % wb_, lambda EE, e=e, wb_=wb_: EE.dma_start(
                            out=w2s[wb_][:].rearrange("p (c d_) -> p c d_", c=2),
                            in_=w2bf[e].rearrange("c p d_ -> p c d_")), 16)
                        p.wait("tensor", tw2)
                        p.wait("tensor", hm_all)
                        if e == 0:
                            p.wait("tensor", gel_last)
                            p.wait("tensor", mk_last)
                        for ti in range(4):
                            for f in range(2):
                                src = Hm if f == 0 else Hm2
                                for dh in range(2):
                                    p.inc("tensor", "sP", lambda EE, e=e, wb_=wb_, f=f,
                                          dh=dh, ti=ti, src=src: EE.matmul(
                                        pz[ti][:, dh * 512:(dh + 1) * 512],
                                        src[:, e * 512 + ti * 128:e * 512 + (ti + 1) * 128],
                                        w2s[wb_][:].rearrange("p (c d_) -> p c d_", c=2)[:, f, dh * 512:(dh + 1) * 512],
                                        start=(e == 0 and f == 0), stop=False))
                        w2ld[wb_] = p.tok("sP")
                    w2_last = p.tok("sP")
                    for ti in range(4):
                        tt = st * 4 + ti
                        p.wait("sync", ep_state["utk_free"][ti])
                        tul = p.inc("sync", "dut%d" % ti, lambda EE, tt=tt, ti=ti: EE.dma_start(
                            out=utk[ti][:], in_=utsp[tt * 128:(tt + 1) * 128, :]), 16)
                        p.wait("tensor", tul)
                        for dh in range(2):
                            p.inc("tensor", "sP", lambda EE, ti=ti, dh=dh: EE.matmul(
                                pz[ti][:, dh * 512:(dh + 1) * 512], idb[:],
                                utk[ti][:, dh * 512:(dh + 1) * 512],
                                start=False, stop=False))
                            p.inc("tensor", "sP", lambda EE, ti=ti, dh=dh, tt=tt: EE.matmul(
                                pz[ti][:, dh * 512:(dh + 1) * 512],
                                selmT[:, tt * 128:(tt + 1) * 128],
                                b2sb[:, dh * 512:(dh + 1) * 512],
                                start=False, stop=True))
                        tz = p.tok("sP")
                        ep_state["utk_free"][ti] = tz
                        zb = ti % 2
                        p.wait("vector", tz)
                        p.wait("vector", zsb_free[zb])
                        tevz = p.inc("vector", "sV", lambda EE, ti=ti, zb=zb: EE.tensor_copy(
                            zsb[zb][:], pz[ti][:]))
                        ep_free[ti] = tevz
                        zsb_free[zb] = emit_epilogue(zsb[zb], tevz, xin, xout, tt,
                                                     gbc, bbc, final)

        def phase1_bufs(ps, sfx):
            xa = [sbIn(ps, "xa%d" % i + sfx, [128, D], F32) for i in range(4)]
            ub = [sbIn(ps, "ub0" + sfx, [128, D], F32), sbIn(ps, "ub1" + sfx, [128, D], F32)]
            uT32 = [sbIn(ps, "uT0" + sfx, [128, D], F32), sbIn(ps, "uT1" + sfx, [128, D], F32)]
            upk = [sbIn(ps, "upk0" + sfx, [128, D], BF16), sbIn(ps, "upk1" + sfx, [128, D], BF16)]
            return xa, ub, uT32, upk

        def ffn_call(xin, xout, final, sfx, post_stats=None):
            with contextlib.ExitStack() as phw:
                fb["tri5"] = sbIn(phw, "tri5" + sfx, [128, 5 * 512], BF16)
                fb["e16"] = sbIn(phw, "e16" + sfx, [16, 2048], BF16)
                p.wait("gpsimd", p.tok("dPreG"))
                t_ = p.inc("gpsimd", "dPreG", lambda EE: EE.dma_start(
                    out=fb["tri5"][:].rearrange("p (a q) -> p a q", a=5),
                    in_=c_tri.rearrange("a p q -> p a q")), 16)
                p.wait("gpsimd", t_)
                t2_ = p.inc("gpsimd", "dPreG", lambda EE: EE.dma_start(
                    out=fb["e16"][:], in_=c_e16), 16)
                p.wait("tensor", t_)
                p.wait("tensor", t2_)
                with contextlib.ExitStack() as ph:
                    xa, ub, uT32, upk = phase1_bufs(ph, sfx)
                    if post_stats is not None:
                        post_stats()
                    phase1(xin, True, sfx, xa, ub, uT32, upk)
                    routing_finalize(sfx)
                    p.barrier()
                ffn_phase2(xin, xout, final, sfx)
                p.barrier()

        def attn_call():
            with contextlib.ExitStack() as ph:
                xa, ub, uT32, upk = phase1_bufs(ph, "A")
                phase1(x1sp, False, "A", xa, ub, uT32, upk)
                p.barrier()
            with contextlib.ExitStack() as ps:
                QT = sbIn(ps, "QT", [128, S], BF16)
                KT = sbIn(ps, "KT", [128, S], BF16)
                Vt = sbIn(ps, "Vt", [128, 32 * 130], BF16)
                Vt3 = Vt[:].rearrange("p (k c) -> p k c", c=130)
                wsl = [sbIn(ps, "wslq", [128, 8 * 128], BF16),
                       sbIn(ps, "wslk", [128, 8 * 128], BF16),
                       sbIn(ps, "wslv", [128, 8 * 128], BF16)]
                biasB = sbIn(ps, "biasB", [128, 2 * 4480], BF16)
                vts = sbIn(ps, "vts", [128, S], BF16)
                qmax = sbIn(ps, "qmax", [128, 16], F32)
                kmax = sbIn(ps, "kmax", [128, 2], F32)
                kmtmp = sbIn(ps, "kmtmp", [128, 1], F32)
                negM = sbIn(ps, "negM", [128, 16], F32)
                mprod = sbIn(ps, "mprod", [128, 16], F32)
                Pt = [[sbIn(ps, "Pa%d" % j, [128, 512], BF16) for j in range(3)],
                      [sbIn(ps, "Pb%d" % j, [128, 512], BF16) for j in range(3)]]
                rec_f = sbIn(ps, "rec_f", [1, 1024], F32)
                rec_b = sbIn(ps, "rec_b", [1, 1024], BF16)
                recbc = sbIn(ps, "recbc", [128, 512], BF16)
                Ostg = sbIn(ps, "Ostg", [128, 512], BF16)
                pS = [[ps.enter_context(nc.psum_tensor("pSa0", [128, 512], F32)),
                       ps.enter_context(nc.psum_tensor("pSa1", [128, 512], F32))],
                      [ps.enter_context(nc.psum_tensor("pSb0", [128, 512], F32)),
                       ps.enter_context(nc.psum_tensor("pSb1", [128, 512], F32))]]
                pO = [ps.enter_context(nc.psum_tensor("pOa", [128, 512], F32)),
                      ps.enter_context(nc.psum_tensor("pOb", [128, 512], F32))]
                pbc = ps.enter_context(nc.psum_tensor("pbcA", [128, 512], F32))
                pvt = ps.enter_context(nc.psum_tensor("pvtA", [128, 1024], BF16))
                tvo = p.inc("gpsimd", "sG", lambda EE: EE.memset(Vt3[:, :, 64:65], 1.0))
                tvo2 = p.inc("gpsimd", "sG", lambda EE: EE.memset(Vt3[:, :, 129:130], 1.0))
                p.wait("tensor", tvo2)
                wfree = None
                prev_bias_use = None
                prev_exp = None
                prev_av = None
                prev_evO = None
                def emit_wsl_loads(prx, waittok):
                    tt_ = []
                    for i3 in range(3):
                        p.wait("gpsimd", waittok)
                        tt_.append(p.inc("gpsimd", "dwsl%d" % i3, lambda EE, prx=prx, i3=i3: EE.dma_start(
                            out=wsl[i3][:].rearrange("p (c f) -> p c f", c=8),
                            in_=wqkvT[i3 * 8 + prx].rearrange("c p f -> p c f")), 16))
                    return tt_

                tws_next = emit_wsl_loads(0, None)
                for pr in range(8):
                    tws = tws_next
                    p.wait("gpsimd", prev_bias_use)
                    tbB = []
                    for hh in range(2):
                        h = 2 * pr + hh
                        tbB.append(p.inc("gpsimd", "draw%d" % hh, lambda EE, h=h, hh=hh: EE.dma_start(
                            out=biasB[:, hh * 4480:(hh + 1) * 4480], in_=cbias[h]), 16))
                    texpB = None
                    ev2 = [None, None]
                    for part in range(3):
                        for t8 in range(NQ):
                            qb = (part * NQ + t8) % 2
                            p.wait("tensor", tws[part])
                            p.wait("tensor", ev2[qb])
                            for c in range(NCH):
                                p.inc("tensor", "sP", lambda EE, part=part, t8=t8, c=c, qb=qb: EE.matmul(
                                    pS[0][qb][:],
                                    wsl[part][:].rearrange("p (c f) -> p c f", c=8)[:, c, :],
                                    u_ch[:, c, t8 * 512:(t8 + 1) * 512],
                                    start=(c == 0), stop=(c == NCH - 1)))
                            tmm = p.tok("sP")
                            p.wait("scalar", tmm)
                            if part == 0:
                                tev = p.inc("scalar", "sA", lambda EE, t8=t8, pr=pr, qb=qb: EE.activation(
                                    QT[:, t8 * 512:(t8 + 1) * 512], pS[0][qb][:], AF.Identity,
                                    bias=wbTq[:, pr:pr + 1], scale=1.0 / 32.0))
                            elif part == 1:
                                tev = p.inc("scalar", "sA", lambda EE, t8=t8, pr=pr, qb=qb: EE.activation(
                                    KT[:, t8 * 512:(t8 + 1) * 512], pS[0][qb][:], AF.Identity,
                                    bias=wbTs[:, 8 + pr:9 + pr]))
                            else:
                                tev = p.inc("scalar", "sA", lambda EE, t8=t8, pr=pr, qb=qb: EE.activation(
                                    vts[:, t8 * 512:(t8 + 1) * 512], pS[0][qb][:], AF.Identity,
                                    bias=wbTs[:, 16 + pr:17 + pr]))
                            ev2[qb] = tev
                    evfree = ev2[(3 * NQ - 1) % 2]
                    tVT = p.tok("sA")
                    vgtok = None
                    for g in range(4):
                        p.wait("tensor", tVT)
                        p.wait("tensor", vgtok)
                        for j in range(8):
                            kt = g * 8 + j
                            p.inc("tensor", "sP", lambda EE, kt=kt, j=j: EE.transpose(
                                pvt[:, j * 128:(j + 1) * 128],
                                vts[:, kt * 128:(kt + 1) * 128], idb[:]))
                        ttr = p.tok("sP")
                        p.wait("vector", ttr)
                        pvt3 = pvt[:].rearrange("p (j f) -> p j f", j=8)
                        p.inc("vector", "sV", lambda EE, g=g, pvt3=pvt3: EE.tensor_copy(
                            Vt3[:, g * 8:(g + 1) * 8, 0:64], pvt3[:, :, 0:64]))
                        vgtok = p.inc("vector", "sV", lambda EE, g=g, pvt3=pvt3: EE.tensor_copy(
                            Vt3[:, g * 8:(g + 1) * 8, 65:129], pvt3[:, :, 64:128]))
                    tVdone = p.tok("sV")
                    if pr < 7:
                        tws_next = emit_wsl_loads(pr + 1, p.tok("sP"))
                    p.wait("scalar", tVdone)
                    tqs = p.inc("scalar", "sA", lambda EE: EE.activation(
                        vts[:], QT[:], AF.Square))
                    mbank = [pbc, pS[1][0]]
                    red2 = [None, None]
                    it = 0
                    for hh in range(2):
                        for qt in range(NQ):
                            mb = it % 2
                            p.wait("tensor", tqs)
                            p.wait("tensor", red2[mb])
                            tmmq = p.inc("tensor", "sP", lambda EE, hh=hh, qt=qt, mb=mb: EE.matmul(
                                mbank[mb][:], ones_bf[hh * 64:(hh + 1) * 64, :],
                                vts[hh * 64:(hh + 1) * 64, qt * 512:(qt + 1) * 512],
                                start=True, stop=True, tile_position=(hh * 64, 0)))
                            p.wait("vector", tmmq)
                            red2[mb] = p.inc("vector", "sV", lambda EE, hh=hh, qt=qt, mb=mb: EE.reduce_max(
                                qmax[:, qt * 2 + hh:qt * 2 + hh + 1], mbank[mb][:], axis=AX.X))
                            it += 1
                    p.wait("scalar", red2[0])
                    p.wait("scalar", red2[1])
                    tks_ = p.inc("scalar", "sA", lambda EE: EE.activation(
                        vts[:], KT[:], AF.Square))
                    for hh in range(2):
                        for kb in range(NQ):
                            mb = it % 2
                            p.wait("tensor", tks_)
                            p.wait("tensor", red2[mb])
                            tmmk = p.inc("tensor", "sP", lambda EE, hh=hh, kb=kb, mb=mb: EE.matmul(
                                mbank[mb][:], ones_bf[hh * 64:(hh + 1) * 64, :],
                                vts[hh * 64:(hh + 1) * 64, kb * 512:(kb + 1) * 512],
                                start=True, stop=True, tile_position=(hh * 64, 0)))
                            p.wait("vector", tmmk)
                            tred = p.inc("vector", "sV", lambda EE, mb=mb: EE.reduce_max(
                                kmtmp[:], mbank[mb][:], axis=AX.X))
                            p.wait("vector", tred)
                            if kb == 0:
                                red2[mb] = p.inc("vector", "sV", lambda EE, hh=hh: EE.tensor_copy(
                                    kmax[:, hh:hh + 1], kmtmp[:]))
                            else:
                                red2[mb] = p.inc("vector", "sV", lambda EE, hh=hh: EE.tensor_tensor(
                                    kmax[:, hh:hh + 1], kmax[:, hh:hh + 1], kmtmp[:], ALU.max))
                            it += 1
                    qm3 = qmax[:].rearrange("p (a two) -> p a two", two=2)
                    mp3 = mprod[:].rearrange("p (a two) -> p a two", two=2)
                    p.wait("vector", red2[0])
                    p.wait("vector", red2[1])
                    for hh in range(2):
                        p.inc("vector", "sV", lambda EE, hh=hh, qm3=qm3, mp3=mp3: EE.tensor_scalar(
                            mp3[:, :, hh:hh + 1], qm3[:, :, hh:hh + 1],
                            kmax[:, hh:hh + 1], None, ALU.mult, ALU.bypass))
                    tmp_ = p.tok("sV")
                    p.wait("gpsimd", tmp_)
                    tpw = p.inc("gpsimd", "sG", lambda EE: EE.tensor_tensor(
                        mprod[:], mprod[:], ph16[:], ALU.pow))
                    p.wait("vector", tpw)
                    tnM = p.inc("vector", "sV", lambda EE: EE.tensor_scalar(
                        negM[:], mprod[:], -1.0, None, ALU.mult, ALU.bypass))
                    p.wait("scalar", tnM)
                    tMprep = red2[1] if it % 2 == 0 else red2[0]

                    steps = [(qt, kt) for qt in range(NQ) for kt in range(4 * qt + 4)]
                    NSTP = len(steps)
                    exp_i = [None] * NSTP
                    av_i = [None] * NSTP
                    ev_q = {-1: prev_evO}

                    def emit_av(j):
                        qtj, ktj = steps[j]
                        nktj = 4 * qtj + 4
                        oqj = max(0, (ktj - 4 * qtj) * 128)
                        p.wait("tensor", exp_i[j])
                        if ktj == 0:
                            p.wait("tensor", ev_q[qtj - 1])
                        for hh in range(2):
                            p.inc("tensor", "sP", lambda EE, hh=hh, ktj=ktj, bkj=j % 3,
                                  nktj=nktj, oqj=oqj: EE.matmul(
                                pO[hh][0:65, oqj:512], Vt3[:, ktj, hh * 65:(hh + 1) * 65],
                                Pt[hh][bkj][:, oqj:512],
                                start=(ktj == 0), stop=(ktj == nktj - 1)))
                        av_i[j] = p.tok("sP")
                        if ktj == nktj - 1:
                            emit_den(qtj, av_i[j])

                    def emit_den(qtd, tav):
                        p.wait("vector", tav)
                        tr1 = [None, None]
                        for hh in range(2):
                            t_r = p.inc("vector", "sV", lambda EE, hh=hh: EE.reciprocal(
                                rec_f[:, hh * 512:(hh + 1) * 512], pO[hh][64:65, :]))
                            p.wait("vector", t_r)
                            tr1[hh] = p.inc("vector", "sV", lambda EE, hh=hh: EE.tensor_copy(
                                rec_b[:, hh * 512:(hh + 1) * 512],
                                rec_f[:, hh * 512:(hh + 1) * 512]))
                        for hh in range(2):
                            p.wait("tensor", tr1[hh])
                            tr2 = p.inc("tensor", "sP", lambda EE, hh=hh: EE.matmul(
                                pbc[:], ones_bf[0:1, :], rec_b[:, hh * 512:(hh + 1) * 512],
                                start=True, stop=True))
                            p.wait("vector", tr2)
                            tr3 = p.inc("vector", "sV", lambda EE: EE.tensor_copy(
                                recbc[:], pbc[:]))
                            p.wait("vector", tr3)
                            tr4 = p.inc("vector", "sV", lambda EE, hh=hh: EE.tensor_tensor(
                                Ostg[hh * 64:(hh + 1) * 64, :], pO[hh][0:64, :],
                                recbc[0:64, :], ALU.mult))
                            p.wait("tensor", tr4)
                        ev_q[qtd] = p.tok("sV")
                        p.wait("gpsimd", ev_q[qtd])
                        tos = p.inc("gpsimd", "dos", lambda EE, pr=pr, qtd=qtd: EE.dma_start(
                            out=osp[pr, 4 * qtd:4 * qtd + 4].rearrange("t p f -> p t f"),
                            in_=Ostg[:].rearrange("p (t f) -> p t f", t=4)), 16)
                        p.wait("vector", tos)

                    for i, (qt, kt) in enumerate(steps):
                        bk = kt % 2
                        oq = max(0, (kt - 4 * qt) * 128)
                        p.wait("tensor", exp_i[i - 2] if i >= 2 else prev_exp)
                        if i < 2:
                            p.wait("tensor", tMprep)
                            p.wait("tensor", tbB[0])
                            p.wait("tensor", tbB[1])
                        for hh in range(2):
                            p.inc("tensor", "sP", lambda EE, hh=hh, qt=qt, kt=kt, bk=bk, oq=oq: EE.matmul(
                                pS[hh][bk][:, oq:512],
                                KT[hh * 64:(hh + 1) * 64, kt * 128:(kt + 1) * 128],
                                QT[hh * 64:(hh + 1) * 64, qt * 512 + oq:(qt + 1) * 512],
                                start=True, stop=False, tile_position=(hh * 64, 0)))
                        for hh in range(2):
                            off = hh * 4480 + (4 * qt - kt + 3) * 128
                            p.inc("tensor", "sP", lambda EE, hh=hh, bk=bk, off=off, oq=oq: EE.matmul(
                                pS[hh][bk][:, oq:512], idb[:], biasB[:, off + oq:off + 512],
                                start=False, stop=True))
                        tsc = p.tok("sP")
                        if i >= 2:
                            emit_av(i - 2)
                        p.wait("scalar", tsc)
                        p.wait("scalar", av_i[i - 3] if i >= 3 else prev_av)
                        for hh in range(2):
                            p.inc("scalar", "sA", lambda EE, hh=hh, qt=qt, bk=bk, bk3=i % 3, oq=oq: EE.activation(
                                Pt[hh][bk3][:, oq:512], pS[hh][bk][:, oq:512], AF.Exp,
                                bias=negM[:, qt * 2 + hh:qt * 2 + hh + 1]))
                        exp_i[i] = p.tok("sA")
                    emit_av(NSTP - 2)
                    emit_av(NSTP - 1)
                    prev_bias_use = p.tok("sP")
                    prev_exp = exp_i[-1]
                    prev_av = av_i[-1]
                    prev_evO = ev_q[NQ - 1]
                    wfree = p.tok("sP")
                p.barrier()

            with contextlib.ExitStack() as ps:
                projTs = sbIn(ps, "projTs", [128, 8 * D], BF16)
                zsbp = [sbIn(ps, "zsbp0", [128, D], F32), sbIn(ps, "zsbp1", [128, D], F32)]
                zsbp_free = [None, None]
                Otl = [sbIn(ps, "Otl0", [128, 8 * 128], BF16),
                       sbIn(ps, "Otl1", [128, 8 * 128], BF16)]
                ppj = ps.enter_context(nc.psum_tensor("ppj", [128, 1024], F32))
                tpj = p.inc("gpsimd", "dpj", lambda EE: EE.dma_start(
                    out=projTs[:].rearrange("p (c d_) -> p c d_", c=NCH),
                    in_=projT.rearrange("c p d_ -> p c d_")), 16)
                tpl = [tpj] * NCH
                otok = [None, None]
                pjfree = None
                for tt in range(NT):
                    b = tt % 2
                    p.wait("sync", otok[b])
                    tol = p.inc("sync", "dot%d" % b, lambda EE, tt=tt, b=b: EE.dma_start(
                        out=Otl[b][:].rearrange("p (c f) -> p c f", c=8),
                        in_=osp[:, tt].rearrange("c p f -> p c f")), 16)
                    p.wait("tensor", tol)
                    p.wait("tensor", pjfree)
                    for c in range(NCH):
                        p.wait("tensor", tpl[c])
                        for dh in range(2):
                            p.inc("tensor", "sP", lambda EE, c=c, dh=dh, b=b: EE.matmul(
                                ppj[:, dh * 512:(dh + 1) * 512],
                                Otl[b][:, c * 128:(c + 1) * 128],
                                projTs[:, c * D + dh * 512:c * D + (dh + 1) * 512],
                                start=(c == 0), stop=False))
                    ui = tt % 4
                    p.wait("sync", ep_state["utk_free"][ui])
                    tul = p.inc("sync", "dut%d" % ui, lambda EE, tt=tt, ui=ui: EE.dma_start(
                        out=utk[ui][:], in_=utsp[tt * 128:(tt + 1) * 128, :]), 16)
                    p.wait("tensor", tul)
                    for dh in range(2):
                        p.inc("tensor", "sP", lambda EE, dh=dh, ui=ui: EE.matmul(
                            ppj[:, dh * 512:(dh + 1) * 512], idb[:],
                            utk[ui][:, dh * 512:(dh + 1) * 512], start=False, stop=False))
                        p.inc("tensor", "sP", lambda EE, dh=dh: EE.matmul(
                            ppj[:, dh * 512:(dh + 1) * 512], idb[:],
                            pjbc[:, dh * 512:(dh + 1) * 512], start=False, stop=True))
                    tz = p.tok("sP")
                    ep_state["utk_free"][ui] = tz
                    otok[b] = tz
                    p.wait("vector", tz)
                    p.wait("vector", zsbp_free[b])
                    tevz = p.inc("vector", "sV", lambda EE, b=b: EE.tensor_copy(
                        zsbp[b][:], ppj[:]))
                    pjfree = tevz
                    zsbp_free[b] = emit_epilogue(zsbp[b], tevz, x1sp, x2sp, tt,
                                                 lgbc, lbbc, False)
            p.barrier()

        ffn_call(x0, x1sp, False, "F1", post_stats=emit_prepass)
        attn_call()
        ffn_call(x2sp, out, True, "F2")
        p.wait("gpsimd", p.tok("dxo0"))
        p.wait("gpsimd", p.tok("dxo1"))

        block = es.enter_context(nc.Block())

        @block.sync
        def _(EE):
            for fn in p.q["sync"]:
                fn(EE)

        @block.gpsimd
        def _(EE):
            for fn in p.q["gpsimd"]:
                fn(EE)

        @block.tensor
        def _(EE):
            for fn in p.q["tensor"]:
                fn(EE)

        @block.vector
        def _(EE):
            for fn in p.q["vector"]:
                fn(EE)

        @block.scalar
        def _(EE):
            for fn in p.q["scalar"]:
                fn(EE)

    return nc


def host_layouts(ins):
    W_w = np.ascontiguousarray(ins["L0_attn_W_w"], np.float32)
    wqkvT = np.ascontiguousarray(W_w.reshape(24, 128, 8, 128).transpose(0, 2, 3, 1))
    wbT = np.ascontiguousarray(np.asarray(ins["L0_attn_W_b"], np.float32).reshape(24, 128).T)
    proj_w = np.asarray(ins["L0_attn_proj_w"], np.float32)
    projT = np.ascontiguousarray(proj_w.reshape(1024, 8, 128).transpose(1, 2, 0))
    rpb = np.asarray(ins["L0_attn_rpb"], np.float32)
    kk = np.arange(128)[:, None, None]
    bb = np.arange(35)[None, :, None]
    qq = np.arange(128)[None, None, :]
    midx = kk - qq - 128 * (bb - 3) + 4095
    valid = midx <= 4095
    mclip = np.clip(midx, 0, 4094)
    cb = rpb[mclip]
    cb = np.where(valid[..., None], cb, np.float32(-1e30))
    cbias = np.ascontiguousarray(cb.transpose(3, 0, 1, 2).reshape(16, 128, 35 * 128))
    wg = np.ascontiguousarray(np.asarray(ins["L0_ffn_wg"], np.float32).reshape(8, 128, 16))
    w1 = np.ascontiguousarray(np.asarray(ins["L0_ffn_w1"], np.float32).reshape(16, 8, 128, 256))
    b1T = np.ascontiguousarray(np.asarray(ins["L0_ffn_b1"], np.float32)
                               .reshape(16, 2, 128).transpose(2, 0, 1).reshape(128, 32))
    w2 = np.ascontiguousarray(np.asarray(ins["L0_ffn_w2"], np.float32).reshape(16, 2, 128, 1024))
    c_idf = np.eye(128, dtype=np.float32)
    r = np.arange(128)[:, None]
    q = np.arange(512)[None, :]
    c_tri = np.zeros((5, 128, 512), np.float32)
    for j in range(4):
        c_tri[j] = (128 * j + r <= q)
    c_tri[4] = 1.0
    c_e16 = np.zeros((16, 2048), np.float32)
    for e in range(16):
        c_e16[e, e * 128:(e + 1) * 128] = 1.0
    com = {
        "wqkvT": wqkvT, "wbT": wbT, "projT": projT,
        "projb": np.asarray(ins["L0_attn_proj_b"], np.float32).reshape(1, 1024),
        "lng": np.asarray(ins["L0_attn_ln_g"], np.float32).reshape(1, 1024),
        "lnb": np.asarray(ins["L0_attn_ln_b"], np.float32).reshape(1, 1024),
        "cbias": cbias, "wg": wg,
        "bg": np.asarray(ins["L0_ffn_bg"], np.float32).reshape(1, 16),
        "w1": w1, "b1T": b1T, "w2": w2,
        "b2": np.asarray(ins["L0_ffn_b2"], np.float32),
        "gam": np.asarray(ins["L0_ffn_gamma"], np.float32).reshape(1, 1024),
        "bet": np.asarray(ins["L0_ffn_beta"], np.float32).reshape(1, 1024),
        "c_idf": c_idf, "c_tri": c_tri, "c_e16": c_e16,
    }
    return {k: np.ascontiguousarray(v, dtype=np.float32) for k, v in com.items()}


def kernel(**inputs):
    com = host_layouts(inputs)
    x = np.asarray(inputs["L0_attn_x"], np.float32)
    nc = build_nc()
    in_maps = []
    for b in range(8):
        m = dict(com)
        m["x0"] = np.ascontiguousarray(x[b])
        in_maps.append(m)
    res = run_bass_kernel_spmd(nc, in_maps, core_ids=list(range(8)))
    return np.stack([r["out"] for r in res.results], axis=0)
